# version 08 (final)
# baseline (speedup 1.0000x reference)
import numpy as np
import ml_dtypes
from contextlib import ExitStack

import concourse.bass as bass
import concourse.mybir as mybir
from concourse.bass_utils import run_bass_kernel_spmd

F32 = mybir.dt.float32
BF16 = mybir.dt.bfloat16
AF = mybir.ActivationFunctionType
ALU = mybir.AluOpType
AX = mybir.AxisListType
bf16 = ml_dtypes.bfloat16

B, S, D, H, DH = 2, 8192, 1024, 8, 128
NCORES = 8
T = 1024
C = 256
NCH = T // C
NSUB = T // 128
KC = D // 128
GAMMAS = [1.0 - 2.0 ** (-5 - h) for h in range(H)]
SQD = float(np.sqrt(DH))
LN_EPS = 1e-5
GN_EPS = 1e-6
FFN_EPS = 1e-8
WSLICE = 6 * D // NCORES


def host_consts():
    ident = np.eye(128, dtype=np.float32)
    dtp = np.zeros((H, 128, 384), dtype=np.float32)
    for h in range(H):
        g = GAMMAS[h]
        p = np.arange(128)
        n = np.arange(256)
        m0 = (n[None, :] >= p[:, None]).astype(np.float64)
        dtp[h, :, 0:256] = (m0 * (g ** (-(p[:, None] + 1.0))) / SQD).astype(np.float32)
        nn = np.arange(128)
        m1 = (nn[None, :] >= p[:, None]).astype(np.float64)
        dtp[h, :, 256:384] = (m1 * (g ** (-(129.0 + p[:, None]))) / SQD).astype(np.float32)
    dk = np.zeros((128, 16), dtype=np.float32)
    for ms in range(2):
        for h in range(H):
            g = GAMMAS[h]
            l = ms * 128 + np.arange(128).astype(np.float64)
            dk[:, ms * 8 + h] = ((g ** (255.0 - l)) / SQD).astype(np.float32)
    lam = np.zeros((H, 128, 256), dtype=np.float32)
    for h in range(H):
        g = GAMMAS[h]
        lam[h] = (g ** (np.arange(256).astype(np.float64) + 1.0))[None, :].astype(np.float32)
    return ident, dtp, dk, lam


def host_cfac(core_id):
    f = np.zeros((64,), dtype=np.float32)
    for j in range(NCORES):
        for h in range(H):
            if j < core_id:
                f[j * 8 + h] = np.float32(GAMMAS[h] ** (float(T) * (core_id - 1 - j)))
    return np.broadcast_to(f, (128, 64)).copy()


class Sem:
    def __init__(self, handle, name):
        self.h = handle
        self.name = name
        self.n = 0

    def bump(self, k):
        self.n += k
        return self.n


class Prog:
    ENGS = ["sync", "gpsimd", "scalar", "vector", "tensor"]

    def __init__(self):
        self.steps = []

    def step(self, eng, fn, waits=(), inc=None, inc_n=1, rd=(), wr=()):
        v = inc.bump(inc_n) if inc is not None else None
        self.steps.append((eng, fn, [(s, w) for (s, w) in waits if w], inc, inc_n, tuple(rd), tuple(wr), v))
        return v

    def check_races(self):
        eng_idx = {e: i for i, e in enumerate(self.ENGS)}
        sem_events = {}
        counts = {}
        step_eng = []
        for sid, (eng, fn, waits, inc, inc_n, rd, wr, v) in enumerate(self.steps):
            step_eng.append(eng)
            if inc is not None:
                nv = counts.get(id(inc), 0) + inc_n
                counts[id(inc)] = nv
                sem_events.setdefault(id(inc), []).append((nv, sid))
        NE = len(self.ENGS)
        vc = [None] * len(self.steps)
        last_on_eng = {e: [-1] * NE for e in self.ENGS}
        eng_prev_vc = {e: [-1] * NE for e in self.ENGS}
        eng_prev_sid = {e: -1 for e in self.ENGS}
        import bisect
        def sem_complete_step(sem, val):
            evs = sem_events.get(id(sem), [])
            lo, hi = 0, len(evs)
            while lo < hi:
                mid = (lo + hi) // 2
                if evs[mid][0] >= val:
                    hi = mid
                else:
                    lo = mid + 1
            return evs[lo][1] if lo < len(evs) else None
        viol = []
        res_hist = {}
        for sid, (eng, fn, waits, inc, inc_n, rd, wr, v) in enumerate(self.steps):
            e = eng_idx[eng]
            myvc = list(eng_prev_vc[eng])
            myvc[e] = max(myvc[e], eng_prev_sid[eng])
            for (s, v) in waits:
                cs = sem_complete_step(s, v)
                if cs is None:
                    viol.append((sid, "UNSATISFIABLE WAIT", s.name, v))
                    continue
                ovc = vc[cs]
                myvc = [max(a, b) for a, b in zip(myvc, ovc)]
                myvc[eng_idx[step_eng[cs]]] = max(myvc[eng_idx[step_eng[cs]]], cs)
            vc[sid] = myvc
            eng_prev_vc[eng] = myvc
            if eng not in ("sync", "gpsimd"):
                eng_prev_sid[eng] = sid
            for r in rd:
                for (osid, typ, oeng) in res_hist.get(r, []):
                    if typ == "W" and oeng != eng and myvc[eng_idx[oeng]] < osid:
                        viol.append((sid, "RAW", r, osid))
            for r in wr:
                for (osid, typ, oeng) in res_hist.get(r, []):
                    if oeng != eng and myvc[eng_idx[oeng]] < osid:
                        viol.append((sid, "W-after-" + typ, r, osid))
            for r in rd:
                res_hist.setdefault(r, []).append((sid, "R", eng))
            for r in wr:
                res_hist.setdefault(r, []).append((sid, "W", eng))
        return viol

    def emit(self, eng_name, e):
        last_wait = {}
        prev = None
        n_fence = 0
        for (eng, fn, waits, inc, inc_n, rd, wr, v) in self.steps:
            if eng != eng_name:
                continue
            fence = None
            if prev is not None and eng_name in ("vector", "scalar"):
                prd, pwr, pinc, pval = prev
                conflict = (set(rd) | set(wr)) & set(pwr) or (set(wr) & set(prd))
                if conflict:
                    assert pinc is not None, (
                        f"fence needed after a step with no inc ({eng_name}, {conflict})")
                    fence = (pinc, pval)
            if fence is not None:
                s_, fv = fence
                if last_wait.get(id(s_), -1) < fv:
                    e.wait_ge(s_.h, fv)
                    last_wait[id(s_)] = fv
                    n_fence += 1
            for (s, wv) in waits:
                if last_wait.get(id(s), -1) >= wv:
                    continue
                e.wait_ge(s.h, wv)
                last_wait[id(s)] = wv
            ins = fn(e)
            if inc is not None:
                assert ins is not None
                ins.then_inc(inc.h, inc_n)
            prev = (rd, wr, inc, v)
        self.fences = getattr(self, "fences", {})
        self.fences[eng_name] = n_fence


def build_nc(stop=None):
    nc = bass.Bass("TRN2", target_bir_lowering=False, detect_race_conditions=False)

    xin = [nc.dram_tensor(f"x{b}", [T, D], F32, kind="ExternalInput").ap() for b in range(B)]
    w_in = {k: nc.dram_tensor(f"w_{k}", [D, D], F32, kind="ExternalInput").ap()
            for k in ["q", "k", "v", "g", "o", "f"]}
    wmods = nc.dram_tensor("wmods", [D, WSLICE], F32, kind="ExternalInput").ap()
    cin = nc.dram_tensor("cin", [B, D], F32, kind="ExternalInput").ap()
    c_ident = nc.dram_tensor("c_ident", [128, 128], F32, kind="ExternalInput").ap()
    c_dtp = nc.dram_tensor("c_dtp", [H, 128, 384], F32, kind="ExternalInput").ap()
    c_dk = nc.dram_tensor("c_dk", [128, 16], F32, kind="ExternalInput").ap()
    c_lam = nc.dram_tensor("c_lam", [H, 128, 256], F32, kind="ExternalInput").ap()
    c_cfac = nc.dram_tensor("c_cfac", [128, 64], F32, kind="ExternalInput").ap()
    c_scal = nc.dram_tensor("c_scal", [128, 12], F32, kind="ExternalInput").ap()
    outd = [nc.dram_tensor(f"out{b}", [T, D], F32, kind="ExternalOutput").ap() for b in range(B)]
    dbg = {}
    if stop == "p2":
        for nm, sh in [("d_y", [T, D]), ("d_g", [T, D]), ("d_xnew", [T, D]), ("d_h2", [T, D]),
                       ("d_aq", [T, D]), ("d_qt", [128, H, T]), ("d_sbf", [NCH, 128, D]),
                       ("d_rr", [T, D]), ("d_gst", [NCH, 128, 64]), ("d_sign", [4, D])]:
            dbg[nm] = nc.dram_tensor(nm, sh, F32, kind="ExternalOutput").ap()
    if stop == "p1":
        for nm, sh in [("d_hfm", [128, KC, T]), ("d_kfm", [128, H, T]), ("d_vtok", [128, NSUB, D]),
                       ("d_sf32", [128, D]), ("d_gath", [NCORES * 128, D]), ("d_mod", [B, 6 * D]),
                       ("d_xa0", [128, D]), ("d_xa0b", [128, D]), ("d_st0", [128, 16]), ("d_h0", [128, D]),
                       ("d_scal", [128, 12])]:
            dbg[nm] = nc.dram_tensor(nm, sh, F32, kind="ExternalOutput").ap()

    cc_mod_in = nc.dram_tensor("cc_mod_in", [B, WSLICE], F32)
    cc_mod_out = nc.dram_tensor("cc_mod_out", [NCORES * B, WSLICE], F32)
    cc_st_in = [nc.dram_tensor(f"cc_st_in{b}", [128, D], BF16) for b in range(B)]
    cc_st_out = [nc.dram_tensor(f"cc_st_out{b}", [NCORES * 128, D], BF16) for b in range(B)]

    es = ExitStack()
    E = es.enter_context
    with es:
        sbuf = lambda n, sh, dt: E(nc.sbuf_tensor(n, sh, dt))
        ident = sbuf("ident", [128, 128], BF16)
        dtp = sbuf("dtp", [128, H, 384], BF16)
        dkcol = sbuf("dkcol", [128, 16], F32)
        lam = sbuf("lam", [128, H, 256], BF16)
        cfac = sbuf("cfac", [128, 64], F32)
        ones_bf = sbuf("ones_bf", [1, 128], BF16)
        ones_f32 = sbuf("ones_f32", [128, 1], F32)
        ones_row = sbuf("ones_row", [1, 128], F32)
        h_fm = sbuf("h_fm", [128, KC, T], BF16)
        k_fm = sbuf("k_fm", [128, H, T], BF16)
        v_tok = sbuf("v_tok", [128, NSUB, D], BF16)
        s_f32 = sbuf("s_f32", [128, D], F32)
        o_save = sbuf("o_save", [128, NCH, D], BF16)
        sbf_all = sbuf("sbf_all", [128, NCH, D], BF16)
        WA = sbuf("WA", [128, KC, D], BF16)
        WB = sbuf("WB", [128, KC, D], BF16)
        WC = sbuf("WC", [128, KC, D], BF16)
        wqt = sbuf("wqt", [128, KC, D], BF16)
        ca1_rep = sbuf("ca1_rep", [128, D], BF16)
        sa_rep = sbuf("sa_rep", [128, D], BF16)
        cb1_rep = sbuf("cb1_rep", [128, D], BF16)
        sb_rep = sbuf("sb_rep", [128, D], BF16)
        siluc_fm = sbuf("siluc_fm", [128, KC, B], BF16)
        xa = sbuf("xa", [128, D], F32)
        xb = sbuf("xb", [128, D], F32)
        xbf = sbuf("xbf", [128, 2, D], BF16)
        wfst = sbuf("wfst", [128, 2, D], F32)
        wmag = wfst[:, 1, :]
        grow = wfst[0:1, 0, :]
        grow2 = xa[0:1, :]
        gb_row = wfst[0:1, 1, :]
        modg = xb[0:1, :]
        stat = sbuf("stat", [128, 16], F32)
        gstat = sbuf("gstat", [128, 64], F32)
        wssc = sbuf("wssc", [128, 8], F32)
        wsc_r = sbuf("wsc_r", [128, 1], F32)
        wsc_n = sbuf("wsc_n", [128, 1], F32)
        wsc_c = sbuf("wsc_c", [128, 1], F32)
        wsc_s = sbuf("wsc_s", [1, 4], F32)
        htok = sbuf("htok", [128, D], BF16)
        ktk = sbuf("ktk", [128, 2, D], BF16)
        modg_bf = ktk[0:1, 0, :]
        qt_ch = sbuf("qt_ch", [128, H, C], BF16)
        g_ch = sbuf("g_ch", [128, 2, D], BF16)
        at_bf = sbuf("at_bf", [128, 2, 384], BF16)
        y_raw = sbuf("y_raw", [128, 2, D], BF16)
        zfm = sbuf("zfm", [128, KC, C], BF16)
        xnew = sbuf("xnew", [128, D], BF16)
        h2c = sbuf("h2c", [128, D], BF16)
        sgn = sbuf("sgn", [128, D], BF16)
        wb16s = sbuf("wb16s", [128, D], BF16)
        aqs = sbuf("aqs", [128, D], BF16)
        dump2_bf = sbuf("dump2_bf", [128, D], BF16)
        wqn = dump2_bf
        cin_sb = xa
        siluc = htok
        modsl = xb
        wmod_bf = v_tok
        g_rep = aqs

        pT = E(nc.psum_tensor("pT", [128, 1024], BF16))
        pW = E(nc.psum_tensor("pW", [128, 512], F32))
        pA = E(nc.psum_tensor("pA", [128, 512], F32))
        pB = E(nc.psum_tensor("pB", [128, 512], F32))
        pR0 = E(nc.psum_tensor("pR0", [128, 1024], F32))
        pR1 = E(nc.psum_tensor("pR1", [128, 1024], F32))

        scal_sb = sbuf("scal_sb", [128, 12], F32)
        _SCAL_VALS = [LN_EPS, GN_EPS, FFN_EPS, 0.5, -0.5, 1.0 / 127.0, 1.0 / float(D * D), 0.25,
                      1.0 / float(D), 1.0 / float(DH), FFN_EPS / 127.0, 2.0]
        for _i, _v in enumerate(_SCAL_VALS):
            nc.const_aps.register(F32, _v, scal_sb[:, _i:_i + 1])

        P = Prog()
        sSY = Sem(E(nc.semaphore("sSY")), "sSY")
        sGP = Sem(E(nc.semaphore("sGP")), "sGP")
        sT = Sem(E(nc.semaphore("sT")), "sT")
        sV = Sem(E(nc.semaphore("sV")), "sV")
        sSC = Sem(E(nc.semaphore("sSC")), "sSC")
        sCC = Sem(E(nc.semaphore("sCC")), "sCC")
        sOUT = Sem(E(nc.semaphore("sOUT")), "sOUT")

        fr = {"pT_S": 0, "pWA_S": 0, "pWA_V": 0, "pB_S": 0}
        ctx = {"last_ffn_T": 0, "last_oproj_T": 0, "last_G_T": 0, "last_k_T": 0, "last_v_T": 0,
               "ret_all_prev": 0, "prev_ln_S": 0, "sbf_used": 0, "pR_OUT": 0, "xbf_T": 0,
               "ffn_T_ns": [0, 0]}

        def gdma(dst, src):
            return lambda g, d=dst, s=src: g.dma_start(out=d, in_=s)

        def sdma(dst, src):
            return lambda sy, d=dst, s=src: sy.dma_start(out=d, in_=s)

        ld_scal = P.step("gpsimd", gdma(scal_sb[:], c_scal), inc=sGP, inc_n=16, wr=["scal"])
        ld_ident = P.step("gpsimd", gdma(ident[:], c_ident), inc=sGP, inc_n=16, wr=["ident"])
        ld_dtp = P.step("gpsimd", gdma(dtp[:], c_dtp.rearrange("h p n -> p h n")), inc=sGP, inc_n=16, wr=["dtp"])
        ld_dk = P.step("gpsimd", gdma(dkcol[:], c_dk), inc=sGP, inc_n=16, wr=["dkc"])
        ld_lam = P.step("gpsimd", gdma(lam[:], c_lam.rearrange("h p n -> p h n")), inc=sGP, inc_n=16, wr=["lam"])
        ld_cfac = P.step("gpsimd", gdma(cfac[:], c_cfac), inc=sGP, inc_n=16, wr=["cfac"])
        ld_cin = P.step("gpsimd", gdma(cin_sb[0:B, :], cin), inc=sGP, inc_n=16, wr=["xa"])
        ld_wmod = P.step("gpsimd", gdma(wmod_bf[:, :, 0:WSLICE], wmods.rearrange("(c p) n -> p c n", p=128)), inc=sGP, inc_n=16, wr=["v_tok"])
        ld_cur = {}
        for k, buf in [("k", WA), ("v", WB), ("q", WC)]:
            ld_cur[k] = P.step("gpsimd", gdma(buf[:], w_in[k].rearrange("(c p) n -> p c n", p=128)), inc=sGP, inc_n=16, wr=[{"k": "WA", "v": "WB", "q": "WC"}[k]])

        P.step("vector", lambda v: v.memset(ones_f32[:], 1.0), inc=sV, wr=["ones"])
        P.step("vector", lambda v: v.memset(ones_row[:], 1.0), inc=sV, wr=["ones"])
        P.step("vector", lambda v: v.memset(ones_bf[:], 1.0), inc=sV, wr=["ones"])
        ones_v = sV.n

        ab = 0
        ldq = 0
        for qtr in range(4):
            ldq = P.step("sync", sdma(wfst[:], w_in["f"][qtr * 256:(qtr + 1) * 256, :].rearrange("(c p) n -> p c n", p=128)),
                         waits=[(sSC, ab)], inc=sSY, inc_n=16, wr=["wfg", "wfg2", "wfm", "wf1", "wf0d"])
            for t in range(2):
                ab = P.step("scalar", lambda s, t=t, qtr=qtr: s.activation(out=htok[:], in_=wfst[:, t, :], func=AF.Abs, accum_out=wssc[:, qtr * 2 + t:qtr * 2 + t + 1]),
                            waits=[(sSY, ldq)], inc=sSC, rd=["wfg", "wfg2", "wfm", "wf1", "wf0d"], wr=["htok", "wssc"])
        fr["wfst_S"] = ab
        vsum = P.step("vector", lambda v: v.reduce_sum(out=wsc_r[:], in_=wssc[:], axis=AX.X), waits=[(sSC, ab)], inc=sV, rd=["wssc"], wr=["wsc"])
        tsum = P.step("tensor", lambda t: t.matmul(pB[0:1, 0:1], lhsT=wsc_r[:], rhs=ones_f32[:], start=True, stop=True),
                      waits=[(sV, vsum), (sV, ones_v)], inc=sT, rd=["wsc", "ones"], wr=["pB"])
        e1 = P.step("scalar", lambda s: s.activation(out=wsc_s[0:1, 0:1], in_=pB[0:1, 0:1], func=AF.Copy,
                                                     scale=1.0 / float(D * D), bias=FFN_EPS), waits=[(sT, tsum), (sGP, ld_scal)], inc=sSC, rd=["pB", "scal"], wr=["wsc"])
        e2 = P.step("scalar", lambda s: s.activation(out=wsc_s[0:1, 1:2], in_=wsc_s[0:1, 0:1], func=AF.Copy, scale=0.5), inc=sSC, rd=["wsc"], wr=["wsc"])
        e3 = P.step("scalar", lambda s: s.activation(out=wsc_s[0:1, 2:3], in_=wsc_s[0:1, 0:1], func=AF.Copy, scale=1.0 / 127.0), inc=sSC, rd=["wsc"], wr=["wsc"])
        e4 = P.step("scalar", lambda s: s.activation(out=wsc_s[0:1, 3:4], in_=wsc_s[0:1, 0:1], func=AF.Copy, scale=-0.5), inc=sSC, rd=["wsc"], wr=["wsc"])
        bl = None
        for (lane, dst) in [(1, wsc_r), (3, wsc_n), (2, wsc_c)]:
            tb = P.step("tensor", lambda t, lane=lane: t.matmul(pB[:, 4 + lane:5 + lane], lhsT=ones_row[0:1, :], rhs=wsc_s[0:1, lane:lane + 1], start=True, stop=True),
                        waits=[(sSC, e4), (sSC, fr["pB_S"])], inc=sT, rd=["ones", "wsc"], wr=["pB"])
            fr["pB_S"] = P.step("scalar", lambda s, lane=lane, dst=dst: s.copy(out=dst[:], in_=pB[:, 4 + lane:5 + lane]), waits=[(sT, tb)], inc=sSC, rd=["pB"], wr=["wsc"])

        m1 = P.step("scalar", lambda s: s.activation(out=siluc[0:B, :], in_=cin_sb[0:B, :], func=AF.Silu),
                    waits=[(sGP, ld_cin)], inc=sSC, rd=["xa"], wr=["htok"])
        mtv = None
        for ch in range(KC):
            mtv = P.step("tensor", lambda t, ch=ch: t.transpose(pT[:, ch * 2:(ch + 1) * 2], in_=siluc[0:B, ch * 128:(ch + 1) * 128], identity=ident[0:B, 0:B]),
                         waits=([(sSC, m1), (sGP, ld_ident), (sSC, fr["pT_S"])] if ch == 0 else []),
                         inc=(sT if ch == KC - 1 else None), rd=["htok", "ident"], wr=["pT"])
        fr["pT_S"] = P.step("scalar", lambda s: s.copy(out=siluc_fm[:].rearrange("p c b -> p (c b)"), in_=pT[:, 0:KC * B]),
                            waits=[(sT, mtv)], inc=sSC, rd=["pT"], wr=["silfm"])
        mm = None
        for piece in range(2):
            for ch in range(KC):
                mm = P.step("tensor", lambda t, piece=piece, ch=ch: t.matmul(
                    (pW if piece == 0 else pA)[0:B, 0:384],
                    lhsT=siluc_fm[:, ch, :], rhs=wmod_bf[:, ch, piece * 384:(piece + 1) * 384],
                    start=(ch == 0), stop=(ch == KC - 1)),
                    waits=([(sSC, fr["pT_S"]), (sGP, ld_wmod)] if (piece == 0 and ch == 0) else []),
                    inc=(sT if ch == KC - 1 else None), rd=["silfm", "v_tok"], wr=["pW", "pA"])
        m3 = P.step("vector", lambda v: (v.tensor_copy(out=modsl[0:B, 0:384], in_=pW[0:B, 0:384]),
                                         v.tensor_copy(out=modsl[0:B, 384:768], in_=pA[0:B, 0:384]))[-1], waits=[(sT, mm)], inc=sV, rd=["pW", "pA"], wr=["xb"])
        fr["pWA_V"] = m3
        m4 = P.step("gpsimd", gdma(cc_mod_in.ap(), modsl[0:B, 0:WSLICE]), waits=[(sV, m3)], inc=sGP, inc_n=16, rd=["xb"], wr=["ccmi"])
        cc1 = P.step("gpsimd", lambda g: g.collective_compute(
            "AllGather", ALU.bypass, replica_groups=[list(range(NCORES))],
            ins=[cc_mod_in.ap()], outs=[cc_mod_out.ap()]),
            waits=[(sGP, m4)], inc=sCC, rd=["ccmi"], wr=["ccmo"])

        def build_wqt(first):
            for qtr in range(4):
                ldh = P.step("sync", sdma(wfst[:], w_in["f"][qtr * 256:(qtr + 1) * 256, :].rearrange("(c p) n -> p c n", p=128)),
                             waits=[(sV, fr.get("wfst_V", 0)), (sSC, fr.get("wfst_S", 0))] + ([(sT, ctx["last_ffn_T"])] if (not first and qtr == 0) else []),
                             inc=sSY, inc_n=16, wr=["wfg", "wfg2", "wfm", "wf1", "wf0d"])
                for t in range(2):
                    oc = qtr * 2 + t
                    a2 = P.step("vector", lambda v, t=t: v.tensor_single_scalar(out=dump2_bf[:], in_=wfst[:, t, :], scalar=wsc_n[:, 0:1], op=ALU.is_le),
                                waits=[(sSY, ldh), (sSC, fr["pB_S"]), (sT, fr.get("wqn_T", 0))], inc=sV,
                                rd=["wfg", "wfg2", "wfm", "wf1", "wf0d", "wsc"], wr=["dump2"])
                    a4 = P.step("vector", lambda v, t=t: v.scalar_tensor_tensor(out=wqn[:], in0=wfst[:, t, :], scalar=wsc_r[:, 0:1], in1=dump2_bf[:], op0=ALU.is_ge, op1=ALU.subtract),
                                inc=sV, rd=["wfg", "wfg2", "wfm", "wf1", "wf0d", "wsc", "dump2"], wr=["dump2"])
                    fr["wfst_V"] = a4
                    tv = None
                    for ic in range(KC):
                        tv = P.step("tensor", lambda t_, ic=ic: t_.transpose(pT[:, ic * 128:(ic + 1) * 128], in_=wqn[:, ic * 128:(ic + 1) * 128], identity=ident[:]),
                                    waits=([(sV, a4), (sSC, fr["pT_S"])] if ic == 0 else []),
                                    inc=(sT if ic == KC - 1 else None), rd=["dump2", "ident"], wr=["pT"])
                    fr["wqn_T"] = tv
                    fr["pT_S"] = P.step("scalar", lambda s, oc=oc: s.copy(
                        out=wqt[:, :, oc * 128:(oc + 1) * 128], in_=pT[:].rearrange("p (c n) -> p c n", n=128)),
                        waits=[(sT, tv)], inc=sSC, rd=["pT"], wr=["wqt"])
            return fr["pT_S"]

        wqt_done = build_wqt(first=True)

        def load_group(grp, bp, want_bf):
            st = (grp * D) // WSLICE
            en = ((grp + 1) * D - 1) // WSLICE
            segs = []
            for sl in range(st, en + 1):
                a = max(grp * D, sl * WSLICE)
                b_ = min((grp + 1) * D, (sl + 1) * WSLICE)
                segs.append((sl, a, b_))
            steps = []
            for (sl, a, b_) in segs:
                if want_bf:
                    dst = modg_bf[0:1, a - grp * D: b_ - grp * D]
                    w_ = [(sCC, cc1), (sT, ctx.get("modgbf_T", 0)),
                          (sT, ctx.get("ktk_free_T", 0)), (sSC, ctx.get("ktk_free_S", 0))]
                else:
                    dst = modg[0:1, a - grp * D: b_ - grp * D]
                    w_ = [(sCC, cc1), (sV, ctx.get("modg_V", 0)), (sSC, ctx.get("modg_S", 0)),
                          (sSC, ctx.get("prev_ln_S", 0)), (sV, ctx.get("htok_V", 0))]
                srcp = cc_mod_out.ap()[2 * sl + bp: 2 * sl + bp + 1, a - sl * WSLICE: b_ - sl * WSLICE]
                steps.append(P.step("gpsimd", gdma(dst, srcp), waits=w_, inc=sGP, inc_n=16,
                                    rd=["ccmo"], wr=["modgbf" if want_bf else "xb"]))
            return steps[-1]

        def phase(bp, run_p2=True):
            gam_c = [float(g ** C) for g in GAMMAS]
            wqnT0 = fr.get("wqn_T", 0)

            if bp > 0:
                ld_cur["k"] = P.step("gpsimd", gdma(WA[:], w_in["k"].rearrange("(c p) n -> p c n", p=128)),
                                     waits=[(sT, ctx["last_oproj_T"])], inc=sGP, inc_n=16, wr=["WA"])
                ld_cur["v"] = P.step("gpsimd", gdma(WB[:], w_in["v"].rearrange("(c p) n -> p c n", p=128)),
                                     waits=[(sT, ctx["last_G_T"])], inc=sGP, inc_n=16, wr=["WB"])

            repname = {id(ca1_rep): "ca1", id(sa_rep): "sa", id(cb1_rep): "cb1", id(sb_rep): "sb"}
            for (dstt, grp, bias) in [(ca1_rep, 1, 1.0), (sa_rep, 0, 0.0), (cb1_rep, 4, 1.0), (sb_rep, 3, 0.0)]:
                mdb = load_group(grp, bp, want_bf=True)
                for hnf in range(2):
                    tv = P.step("tensor", lambda t, hnf=hnf: t.matmul(
                        pB[:, 0:512],
                        lhsT=ones_bf[0:1, :], rhs=modg_bf[0:1, hnf * 512:(hnf + 1) * 512],
                        start=True, stop=True),
                        waits=[(sGP, mdb), (sV, ones_v), (sSC, fr["pB_S"])], inc=sT,
                        rd=["modgbf", "ones"], wr=["pB"])
                    ctx["modgbf_T"] = tv
                    fr["pB_S"] = P.step("scalar", lambda s, dstt=dstt, bias=bias, hnf=hnf: s.activation(
                        out=dstt[:, hnf * 512:(hnf + 1) * 512], in_=pB[:, 0:512], func=AF.Copy, bias=bias),
                        waits=[(sT, tv)], inc=sSC, rd=["pB"], wr=[repname[id(dstt)]])
                fr["pWA_S"] = fr["pB_S"]

            SF_ALL = ["sf%d" % k for k in range(H)]
            zv = P.step("vector", lambda v: v.memset(s_f32[:], 0.0), waits=[(sV, ctx["sbf_used"])], inc=sV, wr=SF_ALL)
            xbufs = [xa, xb]
            xld = [None] * NSUB
            hdone = [None] * NSUB
            vdone = [None] * NSUB
            chain_v = zv
            last_k_T = 0
            last_v_T = 0

            for sub in range(2):
                xld[sub] = P.step("sync", lambda sy, sub=sub, bp=bp: sy.dma_start(out=xbufs[sub % 2][:], in_=xin[bp][sub * 128:(sub + 1) * 128, :]),
                                  waits=[(sSC, (m1 if bp == 0 else ctx["prev_ln_S"]))] +
                                        ([(sGP, m4)] if (bp == 0 and sub == 1) else []) +
                                        ([(sV, ctx.get("modg_V", 0)), (sSC, ctx.get("modg_S", 0))] if sub == 1 else []),
                                  inc=sSY, inc_n=16,
                                  wr=["xa" if sub % 2 == 0 else "xb"])

            for ii in range(NCH):
                for ms in range(2):
                    sub = 2 * ii + ms
                    xt = xbufs[sub % 2]
                    c1 = P.step("scalar", lambda s, xt=xt: s.activation(out=htok[:], in_=xt[:], func=AF.Copy, accum_out=stat[:, 0:1]),
                                waits=[(sSY, xld[sub]), (sV, ctx.get("htok_V", 0)), (sT, ctx.get("htok_T", 0)),
                                       (sV, fr.get("wfst_V", 0) if (bp == 0 and sub == 0) else 0)], inc=sSC,
                                rd=["xa" if sub % 2 == 0 else "xb"], wr=["htok", "st1"])
                    c2 = P.step("vector", lambda v, xt=xt: v.scalar_tensor_tensor(out=dump2_bf[:], in0=xt[:], scalar=1.0, in1=xt[:], op0=ALU.mult, op1=ALU.mult, accum_out=stat[:, 1:2]),
                                waits=[(sSY, xld[sub]), (sSC, c1)] + ([(sT, wqnT0)] if sub == 0 else []), inc=sV,
                                rd=["xa" if sub % 2 == 0 else "xb"], wr=["dump2", "st1"])
                    P.step("vector", lambda v: v.scalar_tensor_tensor(out=stat[:, 2:3], in0=stat[:, 0:1], scalar=1.0 / D, in1=stat[:, 0:1], op0=ALU.mult, op1=ALU.mult), inc=sV, rd=["st1"], wr=["st1"])
                    c5 = P.step("vector", lambda v: v.scalar_tensor_tensor(out=stat[:, 3:4], in0=stat[:, 1:2], scalar=1.0, in1=stat[:, 2:3], op0=ALU.mult, op1=ALU.subtract), inc=sV, rd=["st1"], wr=["st1"])
                    c6 = P.step("scalar", lambda s: s.activation(out=stat[:, 4:5], in_=stat[:, 3:4], func=AF.Sqrt, scale=1.0 / float(D), bias=LN_EPS),
                                waits=[(sV, c5), (sGP, ld_scal)], inc=sSC, rd=["st1", "scal"], wr=["st1"])
                    c7 = P.step("vector", lambda v: v.reciprocal(out=stat[:, 4:5], in_=stat[:, 4:5]),
                                waits=[(sSC, c6)], inc=sV, rd=["st1"], wr=["st1"])
                    c8 = P.step("vector", lambda v: v.scalar_tensor_tensor(out=stat[:, 5:6], in0=stat[:, 0:1], scalar=-1.0 / D, in1=stat[:, 4:5], op0=ALU.mult, op1=ALU.mult),
                                inc=sV, rd=["st1"], wr=["st1"])
                    c9 = P.step("scalar", lambda s, xt=xt: s.activation(out=htok[:], in_=xt[:], func=AF.Identity, scale=stat[:, 4:5], bias=stat[:, 5:6]),
                                waits=[(sV, c8)], inc=sSC, rd=["xa" if sub % 2 == 0 else "xb", "st1"], wr=["htok"])
                    ctx["prev_ln_S"] = c9
                    dbg_gate = []
                    if stop == "p1" and bp == 0 and sub == 0:
                        dxa = P.step("gpsimd", gdma(dbg["d_xa0"], xt[:]), waits=[(sV, c2)], inc=sGP, inc_n=16, rd=["xa"])
                        P.step("sync", sdma(dbg["d_scal"], scal_sb[:]), waits=[(sSC, c6)], inc=sOUT, inc_n=16, rd=["scal"])
                        dst_ = P.step("gpsimd", gdma(dbg["d_st0"], stat[:]), waits=[(sV, c8)], inc=sGP, inc_n=16, rd=["st1"])
                        dxb = P.step("gpsimd", gdma(dbg["d_xa0b"], xt[:]), waits=[(sSC, c9)], inc=sGP, inc_n=16, rd=["xa"])
                        dh0 = P.step("gpsimd", gdma(dbg["d_h0"], htok[:]), waits=[(sSC, c9)], inc=sGP, inc_n=16, rd=["htok"])
                        dbg_gate = [(sGP, dh0)]
                        ctx["p1dbg_gate"] = dh0
                    cA = P.step("vector", lambda v: v.tensor_mul(out=htok[:], in0=htok[:], in1=ca1_rep[:]),
                                waits=[(sSC, c9), (sSC, fr["pWA_S"])] + dbg_gate, inc=sV, rd=["htok", "ca1"], wr=["htok"])
                    cB = P.step("vector", lambda v: v.tensor_add(out=htok[:], in0=htok[:], in1=sa_rep[:]), inc=sV, rd=["htok", "sa"], wr=["htok"])
                    ctx["htok_V"] = cB
                    tv = None
                    for ch in range(KC):
                        tv = P.step("tensor", lambda t, ch=ch: t.transpose(pT[:, ch * 128:(ch + 1) * 128], in_=htok[:, ch * 128:(ch + 1) * 128], identity=ident[:]),
                                    waits=([(sV, cB), (sSC, fr["pT_S"])] if ch == 0 else []),
                                    inc=(sT if ch == KC - 1 else None), rd=["htok", "ident"], wr=["pT"])
                    ctx["htok_T"] = tv
                    hdone[sub] = P.step("scalar", lambda s, sub=sub: s.copy(
                        out=h_fm[:, :, sub * 128:(sub + 1) * 128], in_=pT[:].rearrange("p (c n) -> p c n", n=128)),
                        waits=[(sT, tv)], inc=sSC, rd=["pT"], wr=["h_fm"])
                    fr["pT_S"] = hdone[sub]
                    if sub + 2 < NSUB:
                        xld[sub + 2] = P.step("sync", lambda sy, sub=sub, bp=bp: sy.dma_start(
                            out=xbufs[sub % 2][:], in_=xin[bp][(sub + 2) * 128:(sub + 3) * 128, :]),
                            waits=[(sSC, hdone[sub])], inc=sSY, inc_n=16,
                            wr=["xa" if sub % 2 == 0 else "xb"])
                    pv = pR0 if (sub % 2 == 0) else pR1
                    mmv = None
                    for half in range(2):
                        for ch in range(KC):
                            mmv = P.step("tensor", lambda t, half=half, ch=ch, sub=sub, pv=pv: t.matmul(
                                pv[:, half * 512:(half + 1) * 512],
                                lhsT=h_fm[:, ch, sub * 128:(sub + 1) * 128],
                                rhs=WB[:, ch, half * 512:(half + 1) * 512],
                                start=(ch == 0), stop=(ch == KC - 1)),
                                waits=([(sSC, hdone[sub]), (sGP, ld_cur["v"]),
                                        (sV, vdone[sub - 2] if sub >= 2 else 0),
                                        (sOUT, ctx["pR_OUT"] if sub < 2 else 0)] if (half == 0 and ch == 0) else []),
                                inc=(sT if ch == KC - 1 else None),
                                rd=["h_fm", "WB"], wr=["pR0" if sub % 2 == 0 else "pR1"])
                    last_v_T = mmv
                    vdone[sub] = P.step("vector", lambda v, sub=sub, pv=pv: v.tensor_copy(out=v_tok[:, sub, :], in_=pv[:]),
                                        waits=[(sT, mmv)], inc=sV,
                                        rd=["pR0" if sub % 2 == 0 else "pR1"], wr=["v_tok"])

                kk = None
                kdone_h = {}
                for h in range(H):
                    pk = pW if (h % 2 == 0) else pA
                    for ch in range(KC):
                        kk = P.step("tensor", lambda t, h=h, ch=ch, ii=ii, pk=pk: t.matmul(
                            pk[:, 0:C],
                            lhsT=WA[:, ch, h * 128:(h + 1) * 128],
                            rhs=h_fm[:, ch, ii * C:(ii + 1) * C],
                            start=(ch == 0), stop=(ch == KC - 1)),
                            waits=([(sSC, hdone[2 * ii + 1]), (sGP, ld_cur["k"]),
                                    (sSC, kdone_h.get(h - 2, 0)),
                                    (sV, chain_v if h < 2 else 0), (sSC, fr["pWA_S"] if h < 2 else 0),
                                    (sV, ctx.get("o_ev_prev", 0) if h < 2 else 0),
                                    (sV, fr["pWA_V"] if h < 2 else 0)] if ch == 0 else []),
                            inc=(sT if ch == KC - 1 else None),
                            rd=["h_fm", "WA"], wr=["pW" if h % 2 == 0 else "pA"])
                    kdone_h[h] = P.step("scalar", lambda s, h=h, ii=ii, pk=pk: s.copy(out=k_fm[:, h, ii * C:(ii + 1) * C], in_=pk[:, 0:C]),
                                        waits=[(sT, kk)], inc=sSC,
                                        rd=["pW" if h % 2 == 0 else "pA"], wr=["k_fm"])
                last_k_T = kk
                kev = kdone_h[H - 1]
                ktk_v = 0
                for ms in range(2):
                    tv = None
                    for h in range(H):
                        tv = P.step("tensor", lambda t, h=h, ms=ms, ii=ii: t.transpose(
                            pT[:, h * 128:(h + 1) * 128], in_=k_fm[:, h, ii * C + ms * 128: ii * C + (ms + 1) * 128], identity=ident[:]),
                            waits=([(sSC, kev), (sSC, fr["pT_S"])] if h == 0 else []),
                            inc=(sT if h == H - 1 else None), rd=["k_fm", "ident"], wr=["pT"])
                    kte = None
                    for h in range(H):
                        kte = P.step("scalar", lambda s, ms=ms, h=h: s.activation(
                            out=ktk[:, ms, h * 128:(h + 1) * 128], in_=pT[:, h * 128:(h + 1) * 128],
                            func=AF.Identity, scale=dkcol[:, ms * 8 + h:ms * 8 + h + 1]),
                            waits=([(sT, tv), (sGP, ld_dk)] if h == 0 else []), inc=sSC,
                            rd=["pT", "dkc"], wr=["ktk%d" % ms])
                    fr["pT_S"] = kte
                    ktk_v = kte
                st_mm = None
                for h in range(H):
                    ps_ = pW if h < 4 else pA
                    col = (h % 4) * 128
                    for ms in range(2):
                        st_mm = P.step("tensor", lambda t, h=h, ms=ms, ii=ii, ps_=ps_, col=col: t.matmul(
                            ps_[:, col:col + 128],
                            lhsT=ktk[:, ms, h * 128:(h + 1) * 128],
                            rhs=v_tok[:, 2 * ii + ms, h * 128:(h + 1) * 128],
                            start=(ms == 0), stop=(ms == 1)),
                            waits=([(sSC, ktk_v), (sV, vdone[2 * ii + 1]), (sSC, kev)] if (h == 0 and ms == 0) else []),
                            inc=(sT if ms == 1 else None), rd=["ktk%d" % ms, "v_tok"], wr=["pW" if h < 4 else "pA"])
                P.step("vector", lambda v, ii=ii: v.tensor_copy(out=o_save[:, ii, 0:512], in_=pW[:, 0:512]),
                       waits=[(sT, st_mm)] + ([(sV, ctx.get("osave_read_V", 0))] if ii == 0 else []),
                       inc=sV, rd=["pW"], wr=["osave"])
                o_ev = P.step("vector", lambda v, ii=ii: v.tensor_copy(out=o_save[:, ii, 512:1024], in_=pA[:, 0:512]),
                       inc=sV, rd=["pA"], wr=["osave"])
                ctx["o_ev_prev"] = o_ev
                for h in range(H):
                    ps_ = pW if h < 4 else pA
                    col = (h % 4) * 128
                    chain_v = P.step("vector", lambda v, h=h, ps_=ps_, col=col: v.scalar_tensor_tensor(
                        out=s_f32[:, h * 128:(h + 1) * 128], in0=s_f32[:, h * 128:(h + 1) * 128],
                        scalar=gam_c[h], in1=ps_[:, col:col + 128], op0=ALU.mult, op1=ALU.add),
                        waits=([(sT, st_mm)] if h == 0 else []), inc=sV,
                        rd=["sf%d" % h, "pW" if h < 4 else "pA"], wr=["sf%d" % h])
                fr["pWA_V"] = chain_v

            ctx["last_k_T"] = last_k_T
            ctx["last_v_T"] = last_v_T
            ctx["ktk_free_T"] = st_mm
            ctx["ktk_free_S"] = ktk_v
            ctx["osave_last_V"] = o_ev
            ld_cur["g"] = P.step("gpsimd", gdma(WB[:], w_in["g"].rearrange("(c p) n -> p c n", p=128)),
                                 waits=[(sT, last_v_T)], inc=sGP, inc_n=16)
            def g_eff(dst, grp):
                mdf = load_group(grp, bp, want_bf=False)
                a = P.step("vector", lambda v: v.tensor_single_scalar(out=grow, in_=modg, scalar=0.0, op=ALU.is_ge),
                           waits=[(sGP, mdf)], inc=sV, rd=["xb"], wr=["wfg"])
                P.step("vector", lambda v: v.tensor_scalar(out=grow, in0=grow, scalar1=2.0, scalar2=-1.0, op0=ALU.mult, op1=ALU.add), inc=sV, rd=["wfg"], wr=["wfg"])
                c_ = P.step("scalar", lambda s: s.activation(out=grow2, in_=modg, func=AF.Abs),
                            waits=[(sGP, mdf)], inc=sSC, rd=["xb"], wr=["xa"])
                P.step("vector", lambda v: v.tensor_scalar_max(out=grow2, in0=grow2, scalar1=0.25), waits=[(sSC, c_)], inc=sV, rd=["xa"], wr=["xa"])
                e_ = P.step("vector", lambda v, dst=dst: v.tensor_mul(out=dst[0:1, :], in0=grow, in1=grow2), inc=sV,
                            rd=["wfg", "xa"], wr=["wf1" if dst is gb_row else "wfg"])
                ctx["modg_V"] = e_
                ctx["modg_S"] = c_
                fr["wfst_V"] = max(fr.get("wfst_V", 0), e_)
                fr["wfst_S"] = max(fr.get("wfst_S", 0), c_)
                return e_

            gb_done = g_eff(gb_row, 5)
            if stop == "p2":
                P.step("gpsimd", gdma(dbg["d_sign"][0:1, :], gb_row), waits=[(sV, gb_done)], inc=sGP, inc_n=16)
            ga_done = g_eff(grow, 2)
            if stop == "p2":
                P.step("gpsimd", gdma(dbg["d_sign"][1:2, :], grow), waits=[(sV, ga_done)], inc=sGP, inc_n=16)

            ld_cur["o"] = P.step("gpsimd", gdma(WA[:], w_in["o"].rearrange("(c p) n -> p c n", p=128)),
                                 waits=[(sT, last_k_T)], inc=sGP, inc_n=16, wr=["WA"])
            gae = None
            for hnf in range(2):
                tv = P.step("tensor", lambda t, hnf=hnf: t.matmul(
                    pB[:, 0:512], lhsT=ones_row[0:1, :],
                    rhs=grow[0:1, hnf * 512:(hnf + 1) * 512], start=True, stop=True),
                    waits=[(sV, ga_done), (sSC, fr["pB_S"])], inc=sT,
                    rd=["wfg", "ones"], wr=["pB"])
                gae = P.step("scalar", lambda s, hnf=hnf: s.copy(
                    out=g_rep[:, hnf * 512:(hnf + 1) * 512], in_=pB[:, 0:512]),
                    waits=[(sT, tv)], inc=sSC, rd=["pB"], wr=["aqs"])
                fr["pB_S"] = gae
            wo_fold = 0
            for ch in range(KC):
                wo_fold = P.step("vector", lambda v, ch=ch: v.tensor_mul(out=WA[:, ch, :], in0=WA[:, ch, :], in1=g_rep[:]),
                                 waits=([(sGP, ld_cur["o"]), (sSC, gae)] if ch == 0 else []), inc=sV,
                                 rd=["WA", "aqs"], wr=["WA"])
            gbe = None
            for hnf in range(2):
                tv = P.step("tensor", lambda t, hnf=hnf: t.matmul(
                    pB[:, 0:512], lhsT=ones_row[0:1, :],
                    rhs=gb_row[0:1, hnf * 512:(hnf + 1) * 512], start=True, stop=True),
                    waits=[(sV, gb_done), (sSC, fr["pB_S"]), (sV, wo_fold)], inc=sT, rd=["wf1", "ones"], wr=["pB"])
                gbe = P.step("scalar", lambda s, hnf=hnf: s.copy(
                    out=g_rep[:, hnf * 512:(hnf + 1) * 512], in_=pB[:, 0:512]),
                    waits=[(sT, tv)], inc=sSC, rd=["pB"], wr=["aqs"])
                fr["pB_S"] = gbe
            if stop == "p2":
                P.step("gpsimd", gdma(dbg["d_sign"][2:3, :], g_rep[0:1, :]), waits=[(sSC, gbe)], inc=sGP, inc_n=16)
            wq_fold = 0
            for ch in range(KC):
                wq_fold = P.step("vector", lambda v, ch=ch: v.tensor_mul(out=wqt[:, ch, :], in0=wqt[:, ch, :], in1=g_rep[:]),
                                 waits=([(sSC, gbe), (sSC, wqt_done)] if ch == 0 else []), inc=sV,
                                 rd=["wqt", "aqs"], wr=["wqt"])


            sd1 = P.step("gpsimd", gdma(cc_st_in[bp].ap(), s_f32[:]), waits=[(sV, chain_v)], inc=sGP, inc_n=16,
                         rd=SF_ALL, wr=["ccsi%d" % bp])
            cc2 = P.step("gpsimd", lambda g, bp=bp: g.collective_compute(
                "AllGather", ALU.bypass, replica_groups=[list(range(NCORES))],
                ins=[cc_st_in[bp].ap()], outs=[cc_st_out[bp].ap()]),
                waits=[(sGP, sd1)], inc=sCC, rd=["ccsi%d" % bp], wr=["ccso%d" % bp])

            if not run_p2:
                return cc2


            x2ld = [None] * NSUB
            for sub in range(2):
                x2ld[sub] = P.step("gpsimd", lambda g, sub=sub, bp=bp: g.dma_start(
                    out=xbf[:, sub % 2, :], in_=xin[bp][sub * 128:(sub + 1) * 128, :]),
                    waits=[(sT, ctx["xbf_T"])], inc=sGP, inc_n=16, wr=["xbf%d" % (sub % 2)])

            ffn_done = [0, 0]
            h2s = {"qt_ev": 0, "g_ev": 0, "y_free": 0, "out_ev": 0}

            def emit_qproj(ii):
                for hp in range(4):
                    pq = pW if (hp % 2 == 0) else pA
                    qq = None
                    for h2_ in range(2):
                        h = hp * 2 + h2_
                        for ch in range(KC):
                            qq = P.step("tensor", lambda t, h=h, ch=ch, ii=ii, pq=pq, h2_=h2_: t.matmul(
                                pq[:, h2_ * C:(h2_ + 1) * C],
                                lhsT=WC[:, ch, h * 128:(h + 1) * 128],
                                rhs=h_fm[:, ch, ii * C:(ii + 1) * C],
                                start=(ch == 0), stop=(ch == KC - 1)),
                                waits=([(sGP, ld_cur["q"]),
                                        (sV, h2s["qt_ev"] if hp >= 2 else fr["pWA_V"]),
                                        (sT, ctx["ret_all_prev"] if hp < 2 else 0)] if (h2_ == 0 and ch == 0) else []),
                                inc=(sT if ch == KC - 1 else None),
                                rd=["h_fm", "WC"], wr=["pW" if hp % 2 == 0 else "pA"])
                    h2s["qt_ev"] = P.step("vector", lambda v, hp=hp, pq=pq: v.tensor_tensor(
                        out=qt_ch[:, hp * 2:(hp + 1) * 2, :].rearrange("p h n -> p (h n)"), in0=pq[:, 0:512],
                        in1=lam[:, hp * 2:(hp + 1) * 2, :].rearrange("p h n -> p (h n)"), op=ALU.mult),
                        waits=[(sT, qq), (sGP, ld_lam)], inc=sV,
                        rd=["pW" if hp % 2 == 0 else "pA", "lam"], wr=["qt"])
                fr["pWA_V"] = h2s["qt_ev"]

            def emit_G(ii):
                for ns in range(2):
                    pg = pR0 if ns == 0 else pR1
                    gg = None
                    for half in range(2):
                        for ch in range(KC):
                            gg = P.step("tensor", lambda t, ns=ns, half=half, ch=ch, ii=ii, pg=pg: t.matmul(
                                pg[:, half * 512:(half + 1) * 512],
                                lhsT=h_fm[:, ch, ii * C + ns * 128: ii * C + (ns + 1) * 128],
                                rhs=WB[:, ch, half * 512:(half + 1) * 512],
                                start=(ch == 0), stop=(ch == KC - 1)),
                                waits=([(sGP, ld_cur["g"]), (sOUT, ffn_done[ns]),
                                        (sV, vdone[NSUB - 2 + ns] if ii == 0 else 0),
                                        (sV, h2s["y_free"])] if (half == 0 and ch == 0) else []),
                                inc=(sT if ch == KC - 1 else None),
                                rd=["h_fm", "WB"], wr=["pR0" if ns == 0 else "pR1"])
                    ctx["last_G_T"] = gg
                    h2s["g_ev"] = P.step("scalar", lambda s, ns=ns, pg=pg: s.activation(out=g_ch[:, ns, :], in_=pg[:], func=AF.Silu),
                                  waits=[(sT, gg), (sV, h2s["y_free"])], inc=sSC,
                                  rd=["pR0" if ns == 0 else "pR1"], wr=["gch"])

            emit_qproj(0)
            emit_G(0)
            zv2 = P.step("vector", lambda v: v.memset(s_f32[:], 0.0), waits=[(sGP, sd1)], inc=sV, wr=SF_ALL)
            comb = zv2
            stjv = 0
            for j in range(NCORES):
                stjv = P.step("gpsimd", lambda g, j=j, bp=bp: g.dma_start(
                    out=sbf_all[:, NCH - 1, :], in_=cc_st_out[bp].ap()[j * 128:(j + 1) * 128, :]),
                    waits=[(sCC, cc2), (sGP, ld_cfac), (sV, comb)] +
                          ([(sT, ctx["ret_all_prev"])] if j == 0 else []), inc=sGP, inc_n=16,
                    rd=["ccso%d" % bp], wr=["sbfall"])
                for h in range(H):
                    comb = P.step("vector", lambda v, j=j, h=h: v.scalar_tensor_tensor(
                        out=s_f32[:, h * 128:(h + 1) * 128], in0=sbf_all[:, NCH - 1, h * 128:(h + 1) * 128],
                        scalar=cfac[:, j * 8 + h:j * 8 + h + 1], in1=s_f32[:, h * 128:(h + 1) * 128],
                        op0=ALU.mult, op1=ALU.add),
                        waits=([(sGP, stjv)] if h == 0 else []), inc=sV,
                        rd=["sbfall", "cfac", "sf%d" % h], wr=["sf%d" % h])
            sbf_ready = [0] * NCH
            sbf_ready[0] = P.step("vector", lambda v: v.tensor_copy(out=sbf_all[:, 0, :], in_=s_f32[:]),
                                  waits=[(sV, comb), (sT, ctx["ret_all_prev"])], inc=sV, rd=SF_ALL, wr=["sbfall"])
            chx = comb
            for i2 in range(1, NCH):
                for h in range(H):
                    chx = P.step("vector", lambda v, i2=i2, h=h: v.scalar_tensor_tensor(
                        out=s_f32[:, h * 128:(h + 1) * 128], in0=s_f32[:, h * 128:(h + 1) * 128],
                        scalar=gam_c[h], in1=o_save[:, i2 - 1, h * 128:(h + 1) * 128],
                        op0=ALU.mult, op1=ALU.add),
                        waits=([(sV, sbf_ready[i2 - 1]), (sV, ctx["o_ev_prev"])] if h == 0 else []), inc=sV,
                        rd=["sf%d" % h, "osave"], wr=["sf%d" % h])
                sbf_ready[i2] = P.step("vector", lambda v, i2=i2: v.tensor_copy(out=sbf_all[:, i2, :], in_=s_f32[:]),
                                       waits=[(sV, chx)], inc=sV, rd=SF_ALL, wr=["sbfall"])
            ctx["osave_read_V"] = chx
            ctx["sbf_used"] = sbf_ready[NCH - 1]

            for ii in range(NCH):
                if ii > 0:
                    emit_G(ii)
                qt_ev = h2s["qt_ev"]
                g_ev = h2s["g_ev"]
                out_ev = h2s["out_ev"]
                if stop == "p2" and bp == 0:
                    P.step("gpsimd", gdma(dbg["d_qt"].rearrange("p h t -> p h t")[:, :, ii * C:(ii + 1) * C], qt_ch[:]),
                           waits=[(sV, qt_ev)], inc=sGP, inc_n=16)
                    P.step("gpsimd", gdma(dbg["d_sbf"][ii], sbf_all[:, ii, :]), waits=[(sV, sbf_ready[ii])], inc=sGP, inc_n=16)
                rets = {}
                at_ev = {}
                for h in range(H):
                    psc = pW if (h % 2 == 0) else pA
                    P.step("tensor", lambda t, h=h, ii=ii, psc=psc: t.matmul(
                        psc[:, 0:C], lhsT=k_fm[:, h, ii * C:ii * C + 128], rhs=qt_ch[:, h, :],
                        start=True, stop=True),
                        waits=[(sV, qt_ev), (sV, at_ev.get(h - 2, 0))], inc=sT,
                        rd=["k_fm", "qt"], wr=["pW" if h % 2 == 0 else "pA"])
                    s2_ = P.step("tensor", lambda t, h=h, ii=ii, psc=psc: t.matmul(
                        psc[:, C:C + 128], lhsT=k_fm[:, h, ii * C + 128:ii * C + 256], rhs=qt_ch[:, h, 128:256],
                        start=True, stop=True), inc=sT,
                        rd=["k_fm", "qt"], wr=["pW" if h % 2 == 0 else "pA"])
                    at_ev[h] = P.step("vector", lambda v, h=h, psc=psc: v.tensor_tensor(
                        out=at_bf[:, h % 2, :], in0=psc[:, 0:384], in1=dtp[:, h, :], op=ALU.mult),
                        waits=[(sT, s2_), (sGP, ld_dtp), (sT, rets.get((h - 2, 1), 0))], inc=sV,
                        rd=["pW" if h % 2 == 0 else "pA", "dtp"], wr=["at%d" % (h % 2)])
                    for ns in range(2):
                        pr = pR0 if ns == 0 else pR1
                        blocks = [0] if ns == 0 else [0, 1]
                        first = True
                        for msb in blocks:
                            lh = (ns * 128 if msb == 0 else 256)
                            P.step("tensor", lambda t, h=h, ii=ii, pr=pr, msb=msb, lh=lh, first=first: t.matmul(
                                pr[:, h * 128:(h + 1) * 128],
                                lhsT=at_bf[:, h % 2, lh:lh + 128],
                                rhs=v_tok[:, 2 * ii + msb, h * 128:(h + 1) * 128],
                                start=first, stop=False),
                                waits=([(sV, at_ev[h]), (sV, sbf_ready[ii]), (sSC, g_ev)] if first else []),
                                inc=None, rd=["at%d" % (h % 2), "v_tok"], wr=["pR0" if ns == 0 else "pR1"])
                            first = False
                        rets[(h, ns)] = P.step("tensor", lambda t, h=h, ns=ns, ii=ii, pr=pr: t.matmul(
                            pr[:, h * 128:(h + 1) * 128],
                            lhsT=qt_ch[:, h, ns * 128:(ns + 1) * 128], rhs=sbf_all[:, ii, h * 128:(h + 1) * 128],
                            start=False, stop=True), inc=sT,
                            rd=["qt", "sbfall"], wr=["pR0" if ns == 0 else "pR1"])
                ret_all = rets[(H - 1, 1)]
                ctx["ret_all_prev"] = ret_all
                fr["pWA_V"] = at_ev[H - 1]
                if ii + 1 < NCH:
                    emit_qproj(ii + 1)

                gn_ev = 0
                for ns in range(2):
                    pr = pR0 if ns == 0 else pR1
                    for h in range(H):
                        gn_ev = P.step("scalar", lambda s, ns=ns, h=h, pr=pr: s.activation(
                            out=y_raw[:, ns, h * 128:(h + 1) * 128], in_=pr[:, h * 128:(h + 1) * 128],
                            func=AF.Copy, accum_out=gstat[:, ns * 8 + h:ns * 8 + h + 1]),
                            waits=[(sT, rets[(h, ns)])] + ([(sV, h2s["y_free"])] if (ns == 0 and h == 0) else []), inc=sSC,
                            rd=["pR0" if ns == 0 else "pR1"], wr=["y", "gstat"])
                if stop == "p2" and bp == 0:
                    for ns_ in range(2):
                        P.step("gpsimd", gdma(dbg["d_rr"][(2 * ii + ns_) * 128:(2 * ii + ns_ + 1) * 128, :], y_raw[:, ns_, :]),
                               waits=[(sSC, gn_ev)], inc=sGP, inc_n=16)
                sq_v = 0
                for ns in range(2):
                    for h in range(H):
                        sq_v = P.step("vector", lambda v, ns=ns, h=h: v.scalar_tensor_tensor(
                            out=dump2_bf[:, 0:128], in0=y_raw[:, ns, h * 128:(h + 1) * 128], scalar=1.0,
                            in1=y_raw[:, ns, h * 128:(h + 1) * 128], op0=ALU.mult, op1=ALU.mult,
                            accum_out=gstat[:, 16 + ns * 8 + h:17 + ns * 8 + h]),
                            waits=[(sSC, gn_ev)], inc=sV, rd=["y"], wr=["dump2", "gstat"])
                P.step("vector", lambda v: v.scalar_tensor_tensor(out=gstat[:, 32:48], in0=gstat[:, 0:16], scalar=1.0 / DH, in1=gstat[:, 0:16], op0=ALU.mult, op1=ALU.mult),
                       waits=[(sV, sq_v)], inc=sV, rd=["gstat"], wr=["gstat"])
                g3 = P.step("vector", lambda v: v.scalar_tensor_tensor(out=gstat[:, 48:64], in0=gstat[:, 16:32], scalar=1.0, in1=gstat[:, 32:48], op0=ALU.mult, op1=ALU.subtract), inc=sV, rd=["gstat"], wr=["gstat"])
                g4 = P.step("scalar", lambda s: s.activation(out=gstat[:, 32:48], in_=gstat[:, 48:64], func=AF.Sqrt, scale=1.0 / float(DH), bias=GN_EPS),
                            waits=[(sV, g3), (sGP, ld_scal)], inc=sSC, rd=["gstat", "scal"], wr=["gstat"])
                g5 = P.step("vector", lambda v: v.reciprocal(out=gstat[:, 32:48], in_=gstat[:, 32:48]),
                            waits=[(sSC, g4)], inc=sV, rd=["gstat"], wr=["gstat"])
                g6 = P.step("vector", lambda v: v.scalar_tensor_tensor(out=gstat[:, 48:64], in0=gstat[:, 0:16], scalar=-1.0 / DH, in1=gstat[:, 32:48], op0=ALU.mult, op1=ALU.mult),
                            inc=sV, rd=["gstat"], wr=["gstat"])
                if stop == "p2" and bp == 0:
                    P.step("gpsimd", gdma(dbg["d_gst"][ii], gstat[:]), waits=[(sV, g6)], inc=sGP, inc_n=16)
                yz_v = 0
                yz_ns = {}
                for ns in range(2):
                    for h in range(H):
                        P.step("vector", lambda v, ns=ns, h=h: v.tensor_scalar(
                            out=y_raw[:, ns, h * 128:(h + 1) * 128], in0=y_raw[:, ns, h * 128:(h + 1) * 128],
                            scalar1=gstat[:, 32 + ns * 8 + h:33 + ns * 8 + h], scalar2=gstat[:, 48 + ns * 8 + h:49 + ns * 8 + h],
                            op0=ALU.mult, op1=ALU.add),
                            waits=([(sV, g6)] if (ns == 0 and h == 0) else []), inc=sV, rd=["y", "gstat"], wr=["y"])
                    if stop == "p2" and bp == 0:
                        P.step("gpsimd", gdma(dbg["d_y"][(2 * ii + ns) * 128:(2 * ii + ns + 1) * 128, :], y_raw[:, ns, :]),
                               waits=[(sV, sV.n)], inc=sGP, inc_n=16)
                        P.step("gpsimd", gdma(dbg["d_g"][(2 * ii + ns) * 128:(2 * ii + ns + 1) * 128, :], g_ch[:, ns, :]),
                               waits=[(sSC, g_ev)], inc=sGP, inc_n=16)
                    yz_v = P.step("vector", lambda v, ns=ns: v.tensor_mul(out=y_raw[:, ns, :], in0=y_raw[:, ns, :], in1=g_ch[:, ns, :]),
                                  waits=[(sSC, g_ev)], inc=sV, rd=["y", "gch"], wr=["y"])
                    yz_ns[ns] = yz_v
                h2s["y_free"] = yz_v

                mo_ns = {}
                for ns in range(2):
                    pr = pR0 if ns == 0 else pR1
                    sub = 2 * ii + ns
                    tv = None
                    for ch in range(KC):
                        tv = P.step("tensor", lambda t, ns=ns, ch=ch: t.transpose(
                            pT[:, ch * 128:(ch + 1) * 128], in_=y_raw[:, ns, ch * 128:(ch + 1) * 128], identity=ident[:]),
                            waits=([(sV, yz_ns[ns]), (sSC, fr["pT_S"]), (sT, ctx["ffn_T_ns"][ns])] if ch == 0 else []),
                            inc=(sT if ch == KC - 1 else None), rd=["y", "ident"], wr=["pT"])
                    zfm_s = P.step("scalar", lambda s, ns=ns: s.copy(
                        out=zfm[:, :, ns * 128:(ns + 1) * 128], in_=pT[:].rearrange("p (c n) -> p c n", n=128)),
                        waits=[(sT, tv)], inc=sSC, rd=["pT"], wr=["zfm"])
                    fr["pT_S"] = zfm_s
                    mo = None
                    for half in range(2):
                        for ch in range(KC):
                            mo = P.step("tensor", lambda t, ns=ns, half=half, ch=ch, pr=pr: t.matmul(
                                pr[:, half * 512:(half + 1) * 512],
                                lhsT=zfm[:, ch, ns * 128:(ns + 1) * 128],
                                rhs=WA[:, ch, half * 512:(half + 1) * 512],
                                start=(ch == 0), stop=False),
                                waits=([(sSC, zfm_s), (sV, wo_fold), (sSC, gn_ev), (sV, sq_v)] if (half == 0 and ch == 0) else []),
                                inc=None, rd=["zfm", "WA"], wr=["pR0" if ns == 0 else "pR1"])
                        mo = P.step("tensor", lambda t, ns=ns, half=half, pr=pr, sub=sub: t.matmul(
                            pr[:, half * 512:(half + 1) * 512],
                            lhsT=ident[:], rhs=xbf[:, sub % 2, half * 512:(half + 1) * 512],
                            start=False, stop=True),
                            waits=[(sGP, x2ld[sub])], inc=sT, rd=["ident", "xbf%d" % (sub % 2)], wr=["pR0" if ns == 0 else "pR1"])
                    mo_ns[ns] = mo
                    ctx["last_oproj_T"] = mo
                    ctx["xbf_T"] = mo
                    if sub + 2 < NSUB:
                        x2ld[sub + 2] = P.step("gpsimd", lambda g, sub=sub, bp=bp: g.dma_start(
                            out=xbf[:, sub % 2, :], in_=xin[bp][(sub + 2) * 128:(sub + 3) * 128, :]),
                            waits=[(sT, mo)], inc=sGP, inc_n=16, wr=["xbf%d" % (sub % 2)])
                for ns in range(2):
                    pr = pR0 if ns == 0 else pR1
                    sub = 2 * ii + ns
                    mo = mo_ns[ns]
                    xn = P.step("scalar", lambda s, ns=ns, pr=pr: s.activation(
                        out=xnew[:], in_=pr[:], func=AF.Copy, accum_out=stat[:, 8 + ns:9 + ns]),
                        waits=[(sT, mo), (sT, ctx["last_ffn_T"]), (sV, ctx.get("xnew_V", 0))], inc=sSC,
                        rd=["pR0" if ns == 0 else "pR1"], wr=["xnew", "st2"])
                    if stop == "p2" and bp == 0:
                        P.step("gpsimd", gdma(dbg["d_xnew"][sub * 128:(sub + 1) * 128, :], xnew[:]),
                               waits=[(sSC, xn)], inc=sGP, inc_n=16)
                    s2v = P.step("vector", lambda v, ns=ns: v.scalar_tensor_tensor(
                        out=dump2_bf[:], in0=xnew[:], scalar=1.0, in1=xnew[:], op0=ALU.mult, op1=ALU.mult,
                        accum_out=stat[:, 10 + ns:11 + ns]),
                        waits=[(sSC, xn)], inc=sV, rd=["xnew"], wr=["dump2", "st2"])
                    ctx["xnew_V"] = s2v
                    P.step("vector", lambda v, ns=ns: v.scalar_tensor_tensor(out=stat[:, 12:13], in0=stat[:, 8 + ns:9 + ns], scalar=1.0 / D, in1=stat[:, 8 + ns:9 + ns], op0=ALU.mult, op1=ALU.mult), inc=sV, rd=["st2"], wr=["st2"])
                    l3 = P.step("vector", lambda v, ns=ns: v.scalar_tensor_tensor(out=stat[:, 13:14], in0=stat[:, 10 + ns:11 + ns], scalar=1.0, in1=stat[:, 12:13], op0=ALU.mult, op1=ALU.subtract),
                                waits=[(sV, s2v)], inc=sV, rd=["st2"], wr=["st2"])
                    l4 = P.step("scalar", lambda s: s.activation(out=stat[:, 14:15], in_=stat[:, 13:14], func=AF.Sqrt, scale=1.0 / float(D), bias=LN_EPS), waits=[(sV, l3), (sGP, ld_scal)], inc=sSC, rd=["st2", "scal"], wr=["st2"])
                    l5 = P.step("vector", lambda v: v.reciprocal(out=stat[:, 14:15], in_=stat[:, 14:15]),
                                waits=[(sSC, l4)], inc=sV, rd=["st2"], wr=["st2"])
                    l6 = P.step("vector", lambda v, ns=ns: v.scalar_tensor_tensor(out=stat[:, 15:16], in0=stat[:, 8 + ns:9 + ns], scalar=-1.0 / D, in1=stat[:, 14:15], op0=ALU.mult, op1=ALU.mult),
                                inc=sV, rd=["st2"], wr=["st2"])
                    h21 = P.step("scalar", lambda s, ns=ns: s.activation(out=h2c[:], in_=xnew[:], func=AF.Identity, scale=stat[:, 14:15], bias=stat[:, 15:16]),
                                 waits=[(sV, l6)], inc=sSC, rd=["xnew", "st2"], wr=["h2c"])
                    P.step("vector", lambda v, ns=ns: v.tensor_mul(out=h2c[:], in0=h2c[:], in1=cb1_rep[:]), waits=[(sSC, h21)], inc=sV, rd=["h2c", "cb1"], wr=["h2c"])
                    h23 = P.step("vector", lambda v, ns=ns: v.tensor_add(out=h2c[:], in0=h2c[:], in1=sb_rep[:]), inc=sV, rd=["h2c", "sb"], wr=["h2c"])
                    am = P.step("vector", lambda v, ns=ns: v.reduce_max(out=stat[:, 12:13], in_=h2c[:], axis=AX.X, apply_absolute_value=True),
                                waits=[(sV, h23)], inc=sV, rd=["h2c"], wr=["st2"])
                    P.step("vector", lambda v: v.tensor_scalar_add(out=stat[:, 13:14], in0=stat[:, 12:13], scalar1=FFN_EPS), inc=sV, rd=["st2"], wr=["st2"])
                    P.step("vector", lambda v: v.reciprocal(out=stat[:, 13:14], in_=stat[:, 13:14]), inc=sV, rd=["st2"], wr=["st2"])
                    q4 = P.step("vector", lambda v: v.tensor_scalar_mul(out=stat[:, 13:14], in0=stat[:, 13:14], scalar1=127.0), inc=sV, rd=["st2"], wr=["st2"])
                    q6 = P.step("vector", lambda v: v.scalar_tensor_tensor(out=stat[:, 15:16], in0=stat[:, 12:13], scalar=FFN_EPS, in1=wsc_c[:, 0:1], op0=ALU.add, op1=ALU.mult),
                                waits=[(sSC, fr["pB_S"])], inc=sV, rd=["st2", "wsc"], wr=["st2"])
                    z1 = P.step("scalar", lambda s, ns=ns: s.activation(out=wmag, in_=h2c[:], func=AF.Identity, scale=stat[:, 13:14]),
                                waits=[(sV, q4)], inc=sSC, rd=["h2c", "st2"], wr=["wf1"])
                    z2 = P.step("scalar", lambda s, ns=ns: s.activation(out=sgn[:], in_=h2c[:], func=AF.Sign), inc=sSC, rd=["h2c"], wr=["sgn"])
                    z3 = P.step("vector", lambda v: v.scalar_tensor_tensor(out=wmag, in0=sgn[:], scalar=128.0, in1=wmag, op0=ALU.mult, op1=ALU.add),
                                waits=[(sSC, z2)], inc=sV, rd=["sgn", "wf1"], wr=["wf1"])
                    z4 = P.step("scalar", lambda s: s.copy(out=wb16s[:], in_=wmag), waits=[(sV, z3)], inc=sSC, rd=["wf1"], wr=["wb16"])
                    z5 = P.step("vector", lambda v: v.scalar_tensor_tensor(out=wmag, in0=sgn[:], scalar=-128.0, in1=wb16s[:], op0=ALU.mult, op1=ALU.add),
                                waits=[(sSC, z4)], inc=sV, rd=["sgn", "wb16"], wr=["wf1"])
                    if stop == "p2" and bp == 0:
                        P.step("gpsimd", gdma(dbg["d_h2"][sub * 128:(sub + 1) * 128, :], h2c[:]),
                               waits=[(sV, sV.n)], inc=sGP, inc_n=16)
                    aq_v = P.step("vector", lambda v, ns=ns: v.tensor_scalar(out=aqs[:], in0=wmag, scalar1=stat[:, 15:16], scalar2=None, op0=ALU.mult),
                                  waits=[(sV, q6), (sV, z5)], inc=sV, rd=["wf1", "st2"], wr=["aqs"])
                    fr["wfst_V"] = max(fr.get("wfst_V", 0), aq_v)
                    fr["wfst_S"] = max(fr.get("wfst_S", 0), z4)
                    if stop == "p2" and bp == 0:
                        P.step("gpsimd", gdma(dbg["d_aq"][sub * 128:(sub + 1) * 128, :], aqs[:]),
                               waits=[(sV, aq_v)], inc=sGP, inc_n=16)
                    tv = None
                    for ch in range(KC):
                        tv = P.step("tensor", lambda t, ns=ns, ch=ch: t.transpose(
                            pT[:, ch * 128:(ch + 1) * 128], in_=aqs[:, ch * 128:(ch + 1) * 128], identity=ident[:]),
                            waits=([(sV, aq_v), (sSC, fr["pT_S"])] if ch == 0 else []),
                            inc=(sT if ch == KC - 1 else None), rd=["aqs", "ident"], wr=["pT"])
                    afm_s = P.step("scalar", lambda s, ns=ns: s.copy(
                        out=zfm[:, :, ns * 128:(ns + 1) * 128], in_=pT[:].rearrange("p (c n) -> p c n", n=128)),
                        waits=[(sT, tv)], inc=sSC, rd=["pT"], wr=["zfm"])
                    fr["pT_S"] = afm_s
                    mf = None
                    for half in range(2):
                        for ch in range(KC):
                            mf = P.step("tensor", lambda t, ns=ns, half=half, ch=ch, pr=pr: t.matmul(
                                pr[:, half * 512:(half + 1) * 512],
                                lhsT=zfm[:, ch, ns * 128:(ns + 1) * 128],
                                rhs=wqt[:, ch, half * 512:(half + 1) * 512],
                                start=(ch == 0), stop=False),
                                waits=([(sSC, afm_s), (sV, wq_fold), (sSC, xn), (sV, s2v)] if (half == 0 and ch == 0) else []),
                                inc=None, rd=["zfm", "wqt"], wr=["pR0" if ns == 0 else "pR1"])
                        mf = P.step("tensor", lambda t, ns=ns, half=half, pr=pr: t.matmul(
                            pr[:, half * 512:(half + 1) * 512],
                            lhsT=ident[:], rhs=xnew[:, half * 512:(half + 1) * 512],
                            start=False, stop=True), inc=sT, rd=["ident", "xnew"], wr=["pR0" if ns == 0 else "pR1"])
                    ctx["last_ffn_T"] = mf
                    ctx["ffn_T_ns"][ns] = mf
                    oev = P.step("vector", lambda v, ns=ns, pr=pr: v.tensor_copy(out=wfst[:, 0, :], in_=pr[:]),
                                 waits=[(sT, mf), (sOUT, out_ev), (sSC, gae)], inc=sV,
                                 rd=["pR0" if ns == 0 else "pR1"], wr=["wf0d", "wfg"])
                    fr["wfst_V"] = max(fr.get("wfst_V", 0), oev)
                    out_ev = P.step("sync", lambda sy, sub=sub, bp=bp: sy.dma_start(
                        out=outd[bp][sub * 128:(sub + 1) * 128, :], in_=wfst[:, 0, :]),
                        waits=[(sV, oev)], inc=sOUT, inc_n=16, rd=["wf0d", "wfg"])
                    ffn_done[ns] = out_ev
                    h2s["out_ev"] = out_ev
            ctx["pR_OUT"] = ffn_done[1]
            return cc2

        if stop == "p1":
            cc2 = phase(0, run_p2=False)
            dv = P.step("vector", lambda v: v.tensor_copy(out=wmag, in_=s_f32[:]), waits=[(sV, sV.n)], inc=sV)
            P.step("sync", sdma(dbg["d_sf32"], wmag), waits=[(sV, dv)], inc=sOUT, inc_n=16)
            for nm, t_ in [("d_hfm", h_fm), ("d_kfm", k_fm), ("d_vtok", v_tok)]:
                P.step("gpsimd", gdma(dbg[nm], t_[:]), waits=[(sV, dv)], inc=sGP, inc_n=16)
            P.step("gpsimd", gdma(dbg["d_gath"].rearrange("a b -> a b"), cc_st_out[0].ap()), waits=[(sCC, cc2)], inc=sGP, inc_n=16)
            for b_ in range(B):
                for grp in range(6):
                    st = (grp * D) // WSLICE
                    en = ((grp + 1) * D - 1) // WSLICE
                    for sl in range(st, en + 1):
                        a = max(grp * D, sl * WSLICE)
                        b2 = min((grp + 1) * D, (sl + 1) * WSLICE)
                        P.step("sync", sdma(dbg["d_mod"][b_:b_ + 1, a:b2],
                                            cc_mod_out.ap()[2 * sl + b_: 2 * sl + b_ + 1, a - sl * WSLICE: b2 - sl * WSLICE]),
                               waits=[(sCC, cc1)], inc=sOUT, inc_n=16)
        elif stop == "p2":
            phase(0)
        else:
            phase(0)
            wqt_done = build_wqt(first=False)
            phase(1)

        P.step("sync", lambda sy: sy.nop(), waits=[(sOUT, sOUT.n), (sGP, sGP.n)])

        with nc.Block() as blk:
            blk.sync(lambda e: P.emit("sync", e))
            blk.gpsimd(lambda e: P.emit("gpsimd", e))
            blk.scalar(lambda e: P.emit("scalar", e))
            blk.vector(lambda e: P.emit("vector", e))
            blk.tensor(lambda e: P.emit("tensor", e))
    build_nc.last_prog = P
    return nc


def make_in_maps(L0_attn_X, L0_attn_Wq, L0_attn_Wk, L0_attn_Wv, L0_attn_W_G, L0_attn_W_O,
                 L0_ffn_weight, L0_c, L0_wmod):
    ident, dtp, dk, lam = host_consts()
    X = np.ascontiguousarray(np.asarray(L0_attn_X, dtype=np.float32))
    ws = {"q": L0_attn_Wq, "k": L0_attn_Wk, "v": L0_attn_Wv, "g": L0_attn_W_G, "o": L0_attn_W_O, "f": L0_ffn_weight}
    ws = {k: np.ascontiguousarray(np.asarray(v, dtype=np.float32)) for k, v in ws.items()}
    wmod = np.ascontiguousarray(np.asarray(L0_wmod, dtype=np.float32))
    cc = np.ascontiguousarray(np.asarray(L0_c, dtype=np.float32))
    in_maps = []
    for c in range(NCORES):
        m = {"x0": np.ascontiguousarray(X[0, c * T:(c + 1) * T]),
             "x1": np.ascontiguousarray(X[1, c * T:(c + 1) * T]),
             "wmods": np.ascontiguousarray(wmod[:, c * WSLICE:(c + 1) * WSLICE]),
             "cin": cc, "c_ident": ident, "c_dtp": dtp, "c_dk": dk, "c_lam": lam,
             "c_cfac": host_cfac(c),
             "c_scal": np.broadcast_to(np.array([LN_EPS, GN_EPS, FFN_EPS, 0.5, -0.5, 1.0 / 127.0, 1.0 / float(D * D), 0.25,
                                                 1.0 / float(D), 1.0 / float(DH), FFN_EPS / 127.0, 2.0], dtype=np.float32), (128, 12)).copy()}
        for k in ws:
            m[f"w_{k}"] = ws[k]
        in_maps.append(m)
    return in_maps


def kernel(**inputs):
    in_maps = make_in_maps(**inputs)
    nc = build_nc()
    res = run_bass_kernel_spmd(nc, in_maps, core_ids=list(range(NCORES)))
    out = np.empty((B, S, D), dtype=np.float32)
    for c in range(NCORES):
        out[0, c * T:(c + 1) * T] = res.results[c]["out0"]
        out[1, c * T:(c + 1) * T] = res.results[c]["out1"]
    return out


# version 09
# speedup vs baseline: 1.2008x; 1.2008x over previous
import numpy as np
import ml_dtypes
from contextlib import ExitStack

import concourse.bass as bass
import concourse.mybir as mybir
from concourse.bass_utils import run_bass_kernel_spmd

F32 = mybir.dt.float32
BF16 = mybir.dt.bfloat16
AF = mybir.ActivationFunctionType
ALU = mybir.AluOpType
AX = mybir.AxisListType
bf16 = ml_dtypes.bfloat16

B, S, D, H, DH = 2, 8192, 1024, 8, 128
NCORES = 8
T = 1024
C = 256
NCH = T // C
NSUB = T // 128
KC = D // 128
GAMMAS = [1.0 - 2.0 ** (-5 - h) for h in range(H)]
SQD = float(np.sqrt(DH))
LN_EPS = 1e-5
GN_EPS = 1e-6
FFN_EPS = 1e-8
WSLICE = 6 * D // NCORES


def host_consts():
    ident = np.eye(128, dtype=np.float32)
    dtp = np.zeros((H, 128, 384), dtype=np.float32)
    for h in range(H):
        g = GAMMAS[h]
        p = np.arange(128)
        n = np.arange(256)
        m0 = (n[None, :] >= p[:, None]).astype(np.float64)
        dtp[h, :, 0:256] = (m0 * (g ** (-(p[:, None] + 1.0))) / SQD).astype(np.float32)
        nn = np.arange(128)
        m1 = (nn[None, :] >= p[:, None]).astype(np.float64)
        dtp[h, :, 256:384] = (m1 * (g ** (-(129.0 + p[:, None]))) / SQD).astype(np.float32)
    dk = np.zeros((128, 16), dtype=np.float32)
    for ms in range(2):
        for h in range(H):
            g = GAMMAS[h]
            l = ms * 128 + np.arange(128).astype(np.float64)
            dk[:, ms * 8 + h] = ((g ** (255.0 - l)) / SQD).astype(np.float32)
    lam = np.zeros((H, 128, 256), dtype=np.float32)
    for h in range(H):
        g = GAMMAS[h]
        lam[h] = (g ** (np.arange(256).astype(np.float64) + 1.0))[None, :].astype(np.float32)
    return ident, dtp, dk, lam


def host_cfac(core_id):
    f = np.zeros((64,), dtype=np.float32)
    for j in range(NCORES):
        for h in range(H):
            if j < core_id:
                f[j * 8 + h] = np.float32(GAMMAS[h] ** (float(T) * (core_id - 1 - j)))
    return np.broadcast_to(f, (128, 64)).copy()


class Sem:
    def __init__(self, handle, name):
        self.h = handle
        self.name = name
        self.n = 0

    def bump(self, k):
        self.n += k
        return self.n


class Prog:
    ENGS = ["sync", "gpsimd", "scalar", "vector", "tensor"]

    def __init__(self):
        self.steps = []

    def step(self, eng, fn, waits=(), inc=None, inc_n=1, rd=(), wr=()):
        v = inc.bump(inc_n) if inc is not None else None
        self.steps.append((eng, fn, [(s, w) for (s, w) in waits if w], inc, inc_n, tuple(rd), tuple(wr), v))
        return v

    def check_races(self):
        eng_idx = {e: i for i, e in enumerate(self.ENGS)}
        sem_events = {}
        counts = {}
        step_eng = []
        for sid, (eng, fn, waits, inc, inc_n, rd, wr, v) in enumerate(self.steps):
            step_eng.append(eng)
            if inc is not None:
                nv = counts.get(id(inc), 0) + inc_n
                counts[id(inc)] = nv
                sem_events.setdefault(id(inc), []).append((nv, sid))
        NE = len(self.ENGS)
        vc = [None] * len(self.steps)
        last_on_eng = {e: [-1] * NE for e in self.ENGS}
        eng_prev_vc = {e: [-1] * NE for e in self.ENGS}
        eng_prev_sid = {e: -1 for e in self.ENGS}
        import bisect
        def sem_complete_step(sem, val):
            evs = sem_events.get(id(sem), [])
            lo, hi = 0, len(evs)
            while lo < hi:
                mid = (lo + hi) // 2
                if evs[mid][0] >= val:
                    hi = mid
                else:
                    lo = mid + 1
            return evs[lo][1] if lo < len(evs) else None
        viol = []
        res_hist = {}
        for sid, (eng, fn, waits, inc, inc_n, rd, wr, v) in enumerate(self.steps):
            e = eng_idx[eng]
            myvc = list(eng_prev_vc[eng])
            myvc[e] = max(myvc[e], eng_prev_sid[eng])
            for (s, v) in waits:
                cs = sem_complete_step(s, v)
                if cs is None:
                    viol.append((sid, "UNSATISFIABLE WAIT", s.name, v))
                    continue
                ovc = vc[cs]
                myvc = [max(a, b) for a, b in zip(myvc, ovc)]
                myvc[eng_idx[step_eng[cs]]] = max(myvc[eng_idx[step_eng[cs]]], cs)
            vc[sid] = myvc
            eng_prev_vc[eng] = myvc
            if eng not in ("sync", "gpsimd"):
                eng_prev_sid[eng] = sid
            for r in rd:
                for (osid, typ, oeng) in res_hist.get(r, []):
                    if typ == "W" and oeng != eng and myvc[eng_idx[oeng]] < osid:
                        viol.append((sid, "RAW", r, osid))
            for r in wr:
                for (osid, typ, oeng) in res_hist.get(r, []):
                    if oeng != eng and myvc[eng_idx[oeng]] < osid:
                        viol.append((sid, "W-after-" + typ, r, osid))
            for r in rd:
                res_hist.setdefault(r, []).append((sid, "R", eng))
            for r in wr:
                res_hist.setdefault(r, []).append((sid, "W", eng))
        return viol

    def emit(self, eng_name, e):
        last_wait = {}
        prev = None
        n_fence = 0
        for (eng, fn, waits, inc, inc_n, rd, wr, v) in self.steps:
            if eng != eng_name:
                continue
            fence = None
            if prev is not None and eng_name in ("vector", "scalar"):
                prd, pwr, pinc, pval = prev
                conflict = (set(rd) | set(wr)) & set(pwr) or (set(wr) & set(prd))
                if conflict:
                    assert pinc is not None, (
                        f"fence needed after a step with no inc ({eng_name}, {conflict})")
                    fence = (pinc, pval)
            if fence is not None:
                s_, fv = fence
                if last_wait.get(id(s_), -1) < fv:
                    e.wait_ge(s_.h, fv)
                    last_wait[id(s_)] = fv
                    n_fence += 1
            for (s, wv) in waits:
                if last_wait.get(id(s), -1) >= wv:
                    continue
                e.wait_ge(s.h, wv)
                last_wait[id(s)] = wv
            ins = fn(e)
            if inc is not None:
                assert ins is not None
                ins.then_inc(inc.h, inc_n)
            prev = (rd, wr, inc, v)
        self.fences = getattr(self, "fences", {})
        self.fences[eng_name] = n_fence


def build_nc(stop=None):
    nc = bass.Bass("TRN2", target_bir_lowering=False, detect_race_conditions=False)

    xin = [nc.dram_tensor(f"x{b}", [T, D], F32, kind="ExternalInput").ap() for b in range(B)]
    w_in = {k: nc.dram_tensor(f"w_{k}", [D, D], F32, kind="ExternalInput").ap()
            for k in ["q", "k", "v", "g", "o", "f"]}
    wmods = nc.dram_tensor("wmods", [D, WSLICE], F32, kind="ExternalInput").ap()
    cin = nc.dram_tensor("cin", [B, D], F32, kind="ExternalInput").ap()
    c_ident = nc.dram_tensor("c_ident", [128, 128], F32, kind="ExternalInput").ap()
    c_dtp = nc.dram_tensor("c_dtp", [H, 128, 384], F32, kind="ExternalInput").ap()
    c_dk = nc.dram_tensor("c_dk", [128, 16], F32, kind="ExternalInput").ap()
    c_lam = nc.dram_tensor("c_lam", [H, 128, 256], F32, kind="ExternalInput").ap()
    c_cfac = nc.dram_tensor("c_cfac", [128, 64], F32, kind="ExternalInput").ap()
    c_scal = nc.dram_tensor("c_scal", [128, 12], F32, kind="ExternalInput").ap()
    outd = [nc.dram_tensor(f"out{b}", [T, D], F32, kind="ExternalOutput").ap() for b in range(B)]
    dbg = {}
    if stop == "p2":
        for nm, sh in [("d_y", [T, D]), ("d_g", [T, D]), ("d_xnew", [T, D]), ("d_h2", [T, D]),
                       ("d_aq", [T, D]), ("d_qt", [128, H, T]), ("d_sbf", [NCH, 128, D]),
                       ("d_rr", [T, D]), ("d_gst", [NCH, 128, 64]), ("d_sign", [4, D])]:
            dbg[nm] = nc.dram_tensor(nm, sh, F32, kind="ExternalOutput").ap()
    if stop == "p1":
        for nm, sh in [("d_hfm", [128, KC, T]), ("d_kfm", [128, H, T]), ("d_vtok", [128, NSUB, D]),
                       ("d_sf32", [128, D]), ("d_gath", [NCORES * 128, D]), ("d_mod", [B, 6 * D]),
                       ("d_xa0", [128, D]), ("d_xa0b", [128, D]), ("d_st0", [128, 16]), ("d_h0", [128, D]),
                       ("d_scal", [128, 12])]:
            dbg[nm] = nc.dram_tensor(nm, sh, F32, kind="ExternalOutput").ap()

    cc_mod_in = nc.dram_tensor("cc_mod_in", [B, WSLICE], F32)
    cc_mod_out = nc.dram_tensor("cc_mod_out", [NCORES * B, WSLICE], F32)
    cc_st_in = [nc.dram_tensor(f"cc_st_in{b}", [128, D], BF16) for b in range(B)]
    cc_st_out = [nc.dram_tensor(f"cc_st_out{b}", [NCORES * 128, D], BF16) for b in range(B)]

    es = ExitStack()
    E = es.enter_context
    with es:
        sbuf = lambda n, sh, dt: E(nc.sbuf_tensor(n, sh, dt))
        ident = sbuf("ident", [128, 128], BF16)
        dtp = sbuf("dtp", [128, H, 384], BF16)
        dkcol = sbuf("dkcol", [128, 16], F32)
        lam = sbuf("lam", [128, H, 256], BF16)
        cfac = sbuf("cfac", [128, 64], F32)
        ones_bf = sbuf("ones_bf", [1, 128], BF16)
        ones_f32 = sbuf("ones_f32", [128, 1], F32)
        ones_row = sbuf("ones_row", [1, 128], F32)
        h_fm = sbuf("h_fm", [128, KC, T], BF16)
        k_fm = sbuf("k_fm", [128, H, T], BF16)
        v_tok = sbuf("v_tok", [128, NSUB, D], BF16)
        s_f32 = sbuf("s_f32", [128, D], F32)
        o_save = sbuf("o_save", [128, NCH, D], BF16)
        sbf_all = sbuf("sbf_all", [128, NCH, D], BF16)
        WA = sbuf("WA", [128, KC, D], BF16)
        WB = sbuf("WB", [128, KC, D], BF16)
        WC = sbuf("WC", [128, KC, D], BF16)
        wqt = sbuf("wqt", [128, KC, D], BF16)
        ca1_rep = sbuf("ca1_rep", [128, D], BF16)
        sa_rep = sbuf("sa_rep", [128, D], BF16)
        cb1_rep = sbuf("cb1_rep", [128, D], BF16)
        sb_rep = sbuf("sb_rep", [128, D], BF16)
        siluc_fm = sbuf("siluc_fm", [128, KC, B], BF16)
        xa = sbuf("xa", [128, D], F32)
        xb = sbuf("xb", [128, D], F32)
        xbf = sbuf("xbf", [128, 2, D], BF16)
        wfst = sbuf("wfst", [128, 2, D], F32)
        wmag = wfst[:, 1, :]
        grow = wfst[0:1, 0, :]
        grow2 = xa[0:1, :]
        gb_row = wfst[0:1, 1, :]
        modg = xb[0:1, :]
        stat = sbuf("stat", [128, 16], F32)
        gstat = sbuf("gstat", [128, 64], F32)
        wssc = sbuf("wssc", [128, 8], F32)
        wsc_r = sbuf("wsc_r", [128, 1], F32)
        wsc_n = sbuf("wsc_n", [128, 1], F32)
        wsc_c = sbuf("wsc_c", [128, 1], F32)
        wsc_s = sbuf("wsc_s", [1, 4], F32)
        htok = sbuf("htok", [128, D], BF16)
        ktk = sbuf("ktk", [128, 2, D], BF16)
        modg_bf = ktk[0:1, 0, :]
        qt_ch = sbuf("qt_ch", [128, H, C], BF16)
        g_ch = sbuf("g_ch", [128, 2, D], BF16)
        at_bf = sbuf("at_bf", [128, 2, 384], BF16)
        y_raw = sbuf("y_raw", [128, 2, D], BF16)
        zfm = sbuf("zfm", [128, KC, C], BF16)
        xnew = sbuf("xnew", [128, D], BF16)
        h2c = sbuf("h2c", [128, D], BF16)
        sgn = sbuf("sgn", [128, D], BF16)
        wb16s = sbuf("wb16s", [128, D], BF16)
        aqs = sbuf("aqs", [128, D], BF16)
        dump2_bf = sbuf("dump2_bf", [128, D], BF16)
        wqn = dump2_bf
        cin_sb = xa
        siluc = htok
        modsl = xb
        wmod_bf = v_tok
        g_rep = aqs

        pT = E(nc.psum_tensor("pT", [128, 1024], BF16))
        pW = E(nc.psum_tensor("pW", [128, 512], F32))
        pA = E(nc.psum_tensor("pA", [128, 512], F32))
        pB = E(nc.psum_tensor("pB", [128, 512], F32))
        pR0 = E(nc.psum_tensor("pR0", [128, 1024], F32))
        pR1 = E(nc.psum_tensor("pR1", [128, 1024], F32))

        scal_sb = sbuf("scal_sb", [128, 12], F32)
        _SCAL_VALS = [LN_EPS, GN_EPS, FFN_EPS, 0.5, -0.5, 1.0 / 127.0, 1.0 / float(D * D), 0.25,
                      1.0 / float(D), 1.0 / float(DH), FFN_EPS / 127.0, 2.0]
        for _i, _v in enumerate(_SCAL_VALS):
            nc.const_aps.register(F32, _v, scal_sb[:, _i:_i + 1])

        P = Prog()
        sSY = Sem(E(nc.semaphore("sSY")), "sSY")
        sGP = Sem(E(nc.semaphore("sGP")), "sGP")
        sT = Sem(E(nc.semaphore("sT")), "sT")
        sV = Sem(E(nc.semaphore("sV")), "sV")
        sSC = Sem(E(nc.semaphore("sSC")), "sSC")
        sCC = Sem(E(nc.semaphore("sCC")), "sCC")
        sOUT = Sem(E(nc.semaphore("sOUT")), "sOUT")

        fr = {"pT_S": 0, "pWA_S": 0, "pWA_V": 0, "pB_S": 0}
        ctx = {"last_ffn_T": 0, "last_oproj_T": 0, "last_G_T": 0, "last_k_T": 0, "last_v_T": 0,
               "ret_all_prev": 0, "prev_ln_S": 0, "sbf_used": 0, "pR_OUT": 0, "xbf_T": 0,
               "ffn_T_ns": [0, 0]}

        def gdma(dst, src):
            return lambda g, d=dst, s=src: g.dma_start(out=d, in_=s)

        def sdma(dst, src):
            return lambda sy, d=dst, s=src: sy.dma_start(out=d, in_=s)

        ld_scal = P.step("gpsimd", gdma(scal_sb[:], c_scal), inc=sGP, inc_n=16, wr=["scal"])
        ld_ident = P.step("gpsimd", gdma(ident[:], c_ident), inc=sGP, inc_n=16, wr=["ident"])
        ld_dtp = P.step("gpsimd", gdma(dtp[:], c_dtp.rearrange("h p n -> p h n")), inc=sGP, inc_n=16, wr=["dtp"])
        ld_dk = P.step("gpsimd", gdma(dkcol[:], c_dk), inc=sGP, inc_n=16, wr=["dkc"])
        ld_lam = P.step("gpsimd", gdma(lam[:], c_lam.rearrange("h p n -> p h n")), inc=sGP, inc_n=16, wr=["lam"])
        ld_cfac = P.step("gpsimd", gdma(cfac[:], c_cfac), inc=sGP, inc_n=16, wr=["cfac"])
        ld_cin = P.step("gpsimd", gdma(cin_sb[0:B, :], cin), inc=sGP, inc_n=16, wr=["xa"])
        ld_wmod = P.step("gpsimd", gdma(wmod_bf[:, :, 0:WSLICE], wmods.rearrange("(c p) n -> p c n", p=128)), inc=sGP, inc_n=16, wr=["v_tok"])
        ld_cur = {}
        for k, buf in [("k", WA), ("v", WB), ("q", WC)]:
            ld_cur[k] = P.step("gpsimd", gdma(buf[:], w_in[k].rearrange("(c p) n -> p c n", p=128)), inc=sGP, inc_n=16, wr=[{"k": "WA", "v": "WB", "q": "WC"}[k]])

        P.step("vector", lambda v: v.memset(ones_f32[:], 1.0), inc=sV, wr=["ones"])
        P.step("vector", lambda v: v.memset(ones_row[:], 1.0), inc=sV, wr=["ones"])
        P.step("vector", lambda v: v.memset(ones_bf[:], 1.0), inc=sV, wr=["ones"])
        ones_v = sV.n

        ab = 0
        ldq = 0
        for qtr in range(4):
            ldq = P.step("sync", sdma(wfst[:], w_in["f"][qtr * 256:(qtr + 1) * 256, :].rearrange("(c p) n -> p c n", p=128)),
                         waits=[(sSC, ab)], inc=sSY, inc_n=16, wr=["wfg", "wfg2", "wfm", "wf1", "wf0d"])
            for t in range(2):
                ab = P.step("scalar", lambda s, t=t, qtr=qtr: s.activation(out=htok[:], in_=wfst[:, t, :], func=AF.Abs, accum_out=wssc[:, qtr * 2 + t:qtr * 2 + t + 1]),
                            waits=[(sSY, ldq)], inc=sSC, rd=["wfg", "wfg2", "wfm", "wf1", "wf0d"], wr=["htok", "wssc"])
        fr["wfst_S"] = ab
        vsum = P.step("vector", lambda v: v.reduce_sum(out=wsc_r[:], in_=wssc[:], axis=AX.X), waits=[(sSC, ab)], inc=sV, rd=["wssc"], wr=["wsc"])
        tsum = P.step("tensor", lambda t: t.matmul(pB[0:1, 0:1], lhsT=wsc_r[:], rhs=ones_f32[:], start=True, stop=True),
                      waits=[(sV, vsum), (sV, ones_v)], inc=sT, rd=["wsc", "ones"], wr=["pB"])
        e1 = P.step("scalar", lambda s: s.activation(out=wsc_s[0:1, 0:1], in_=pB[0:1, 0:1], func=AF.Copy,
                                                     scale=1.0 / float(D * D), bias=FFN_EPS), waits=[(sT, tsum), (sGP, ld_scal)], inc=sSC, rd=["pB", "scal"], wr=["wsc"])
        e2 = P.step("scalar", lambda s: s.activation(out=wsc_s[0:1, 1:2], in_=wsc_s[0:1, 0:1], func=AF.Copy, scale=0.5), inc=sSC, rd=["wsc"], wr=["wsc"])
        e3 = P.step("scalar", lambda s: s.activation(out=wsc_s[0:1, 2:3], in_=wsc_s[0:1, 0:1], func=AF.Copy, scale=1.0 / 127.0), inc=sSC, rd=["wsc"], wr=["wsc"])
        e4 = P.step("scalar", lambda s: s.activation(out=wsc_s[0:1, 3:4], in_=wsc_s[0:1, 0:1], func=AF.Copy, scale=-0.5), inc=sSC, rd=["wsc"], wr=["wsc"])
        bl = None
        for (lane, dst) in [(1, wsc_r), (3, wsc_n), (2, wsc_c)]:
            tb = P.step("tensor", lambda t, lane=lane: t.matmul(pB[:, 4 + lane:5 + lane], lhsT=ones_row[0:1, :], rhs=wsc_s[0:1, lane:lane + 1], start=True, stop=True),
                        waits=[(sSC, e4), (sSC, fr["pB_S"])], inc=sT, rd=["ones", "wsc"], wr=["pB"])
            fr["pB_S"] = P.step("scalar", lambda s, lane=lane, dst=dst: s.copy(out=dst[:], in_=pB[:, 4 + lane:5 + lane]), waits=[(sT, tb)], inc=sSC, rd=["pB"], wr=["wsc"])

        m1 = P.step("scalar", lambda s: s.activation(out=siluc[0:B, :], in_=cin_sb[0:B, :], func=AF.Silu),
                    waits=[(sGP, ld_cin)], inc=sSC, rd=["xa"], wr=["htok"])
        mtv = None
        for ch in range(KC):
            mtv = P.step("tensor", lambda t, ch=ch: t.transpose(pT[:, ch * 2:(ch + 1) * 2], in_=siluc[0:B, ch * 128:(ch + 1) * 128], identity=ident[0:B, 0:B]),
                         waits=([(sSC, m1), (sGP, ld_ident), (sSC, fr["pT_S"])] if ch == 0 else []),
                         inc=(sT if ch == KC - 1 else None), rd=["htok", "ident"], wr=["pT"])
        fr["pT_S"] = P.step("scalar", lambda s: s.copy(out=siluc_fm[:].rearrange("p c b -> p (c b)"), in_=pT[:, 0:KC * B]),
                            waits=[(sT, mtv)], inc=sSC, rd=["pT"], wr=["silfm"])
        mm = None
        for piece in range(2):
            for ch in range(KC):
                mm = P.step("tensor", lambda t, piece=piece, ch=ch: t.matmul(
                    (pW if piece == 0 else pA)[0:B, 0:384],
                    lhsT=siluc_fm[:, ch, :], rhs=wmod_bf[:, ch, piece * 384:(piece + 1) * 384],
                    start=(ch == 0), stop=(ch == KC - 1)),
                    waits=([(sSC, fr["pT_S"]), (sGP, ld_wmod)] if (piece == 0 and ch == 0) else []),
                    inc=(sT if ch == KC - 1 else None), rd=["silfm", "v_tok"], wr=["pW", "pA"])
        m3 = P.step("vector", lambda v: (v.tensor_copy(out=modsl[0:B, 0:384], in_=pW[0:B, 0:384]),
                                         v.tensor_copy(out=modsl[0:B, 384:768], in_=pA[0:B, 0:384]))[-1], waits=[(sT, mm)], inc=sV, rd=["pW", "pA"], wr=["xb"])
        fr["pWA_V"] = m3
        m4 = P.step("gpsimd", gdma(cc_mod_in.ap(), modsl[0:B, 0:WSLICE]), waits=[(sV, m3)], inc=sGP, inc_n=16, rd=["xb"], wr=["ccmi"])
        cc1 = P.step("gpsimd", lambda g: g.collective_compute(
            "AllGather", ALU.bypass, replica_groups=[list(range(NCORES))],
            ins=[cc_mod_in.ap()], outs=[cc_mod_out.ap()]),
            waits=[(sGP, m4)], inc=sCC, rd=["ccmi"], wr=["ccmo"])

        def build_wqt(first):
            for qtr in range(4):
                ldh = P.step("sync", sdma(wfst[:], w_in["f"][qtr * 256:(qtr + 1) * 256, :].rearrange("(c p) n -> p c n", p=128)),
                             waits=[(sV, fr.get("wfst_V", 0)), (sSC, fr.get("wfst_S", 0))] + ([(sT, ctx["last_ffn_T"])] if (not first and qtr == 0) else []),
                             inc=sSY, inc_n=16, wr=["wfg", "wfg2", "wfm", "wf1", "wf0d"])
                for t in range(2):
                    oc = qtr * 2 + t
                    a2 = P.step("vector", lambda v, t=t: v.tensor_single_scalar(out=dump2_bf[:], in_=wfst[:, t, :], scalar=wsc_n[:, 0:1], op=ALU.is_le),
                                waits=[(sSY, ldh), (sSC, fr["pB_S"]), (sT, fr.get("wqn_T", 0))], inc=sV,
                                rd=["wfg", "wfg2", "wfm", "wf1", "wf0d", "wsc"], wr=["dump2"])
                    a4 = P.step("vector", lambda v, t=t: v.scalar_tensor_tensor(out=wqn[:], in0=wfst[:, t, :], scalar=wsc_r[:, 0:1], in1=dump2_bf[:], op0=ALU.is_ge, op1=ALU.subtract),
                                inc=sV, rd=["wfg", "wfg2", "wfm", "wf1", "wf0d", "wsc", "dump2"], wr=["dump2"])
                    fr["wfst_V"] = a4
                    tv = None
                    for ic in range(KC):
                        tv = P.step("tensor", lambda t_, ic=ic: t_.transpose(pT[:, ic * 128:(ic + 1) * 128], in_=wqn[:, ic * 128:(ic + 1) * 128], identity=ident[:]),
                                    waits=([(sV, a4), (sSC, fr["pT_S"])] if ic == 0 else []),
                                    inc=(sT if ic == KC - 1 else None), rd=["dump2", "ident"], wr=["pT"])
                    fr["wqn_T"] = tv
                    fr["pT_S"] = P.step("scalar", lambda s, oc=oc: s.copy(
                        out=wqt[:, :, oc * 128:(oc + 1) * 128], in_=pT[:].rearrange("p (c n) -> p c n", n=128)),
                        waits=[(sT, tv)], inc=sSC, rd=["pT"], wr=["wqt"])
            return fr["pT_S"]

        wqt_done = build_wqt(first=True)

        def load_group(grp, bp, want_bf):
            st = (grp * D) // WSLICE
            en = ((grp + 1) * D - 1) // WSLICE
            segs = []
            for sl in range(st, en + 1):
                a = max(grp * D, sl * WSLICE)
                b_ = min((grp + 1) * D, (sl + 1) * WSLICE)
                segs.append((sl, a, b_))
            steps = []
            for (sl, a, b_) in segs:
                if want_bf:
                    dst = modg_bf[0:1, a - grp * D: b_ - grp * D]
                    w_ = [(sCC, cc1), (sT, ctx.get("modgbf_T", 0)),
                          (sT, ctx.get("ktk_free_T", 0)), (sSC, ctx.get("ktk_free_S", 0))]
                else:
                    dst = modg[0:1, a - grp * D: b_ - grp * D]
                    w_ = [(sCC, cc1), (sV, ctx.get("modg_V", 0)), (sSC, ctx.get("modg_S", 0)),
                          (sSC, ctx.get("prev_ln_S", 0)), (sV, ctx.get("htok_V", 0))]
                srcp = cc_mod_out.ap()[2 * sl + bp: 2 * sl + bp + 1, a - sl * WSLICE: b_ - sl * WSLICE]
                steps.append(P.step("gpsimd", gdma(dst, srcp), waits=w_, inc=sGP, inc_n=16,
                                    rd=["ccmo"], wr=["modgbf" if want_bf else "xb"]))
            return steps[-1]

        def phase(bp, run_p2=True):
            gam_c = [float(g ** C) for g in GAMMAS]
            wqnT0 = fr.get("wqn_T", 0)

            if bp > 0:
                ld_cur["k"] = P.step("gpsimd", gdma(WA[:], w_in["k"].rearrange("(c p) n -> p c n", p=128)),
                                     waits=[(sT, ctx["last_oproj_T"])], inc=sGP, inc_n=16, wr=["WA"])
                ld_cur["v"] = P.step("gpsimd", gdma(WB[:], w_in["v"].rearrange("(c p) n -> p c n", p=128)),
                                     waits=[(sT, ctx["last_G_T"])], inc=sGP, inc_n=16, wr=["WB"])

            repname = {id(ca1_rep): "ca1", id(sa_rep): "sa", id(cb1_rep): "cb1", id(sb_rep): "sb"}
            for (dstt, grp, bias) in [(ca1_rep, 1, 1.0), (sa_rep, 0, 0.0), (cb1_rep, 4, 1.0), (sb_rep, 3, 0.0)]:
                mdb = load_group(grp, bp, want_bf=True)
                for hnf in range(2):
                    tv = P.step("tensor", lambda t, hnf=hnf: t.matmul(
                        pB[:, 0:512],
                        lhsT=ones_bf[0:1, :], rhs=modg_bf[0:1, hnf * 512:(hnf + 1) * 512],
                        start=True, stop=True),
                        waits=[(sGP, mdb), (sV, ones_v), (sSC, fr["pB_S"])], inc=sT,
                        rd=["modgbf", "ones"], wr=["pB"])
                    ctx["modgbf_T"] = tv
                    fr["pB_S"] = P.step("scalar", lambda s, dstt=dstt, bias=bias, hnf=hnf: s.activation(
                        out=dstt[:, hnf * 512:(hnf + 1) * 512], in_=pB[:, 0:512], func=AF.Copy, bias=bias),
                        waits=[(sT, tv)], inc=sSC, rd=["pB"], wr=[repname[id(dstt)]])
                fr["pWA_S"] = fr["pB_S"]

            SF_ALL = ["sf%d" % k for k in range(H)]
            zv = P.step("vector", lambda v: v.memset(s_f32[:], 0.0), waits=[(sV, ctx["sbf_used"])], inc=sV, wr=SF_ALL)
            xbufs = [xa, xb]
            xld = [None] * NSUB
            hdone = [None] * NSUB
            vdone = [None] * NSUB
            chain_v = zv
            last_k_T = 0
            last_v_T = 0

            for sub in range(2):
                xld[sub] = P.step("sync", lambda sy, sub=sub, bp=bp: sy.dma_start(out=xbufs[sub % 2][:], in_=xin[bp][sub * 128:(sub + 1) * 128, :]),
                                  waits=[(sSC, (m1 if bp == 0 else ctx["prev_ln_S"]))] +
                                        ([(sGP, m4)] if (bp == 0 and sub == 1) else []) +
                                        ([(sV, ctx.get("modg_V", 0)), (sSC, ctx.get("modg_S", 0))] if sub == 1 else []),
                                  inc=sSY, inc_n=16,
                                  wr=["xa" if sub % 2 == 0 else "xb"])

            for ii in range(NCH):
                for ms in range(2):
                    sub = 2 * ii + ms
                    xt = xbufs[sub % 2]
                    c1 = P.step("scalar", lambda s, xt=xt: s.activation(out=htok[:], in_=xt[:], func=AF.Copy, accum_out=stat[:, 0:1]),
                                waits=[(sSY, xld[sub]), (sV, ctx.get("htok_V", 0)), (sT, ctx.get("htok_T", 0)),
                                       (sV, fr.get("wfst_V", 0) if (bp == 0 and sub == 0) else 0)], inc=sSC,
                                rd=["xa" if sub % 2 == 0 else "xb"], wr=["htok", "st1"])
                    c2 = P.step("vector", lambda v, xt=xt: v.scalar_tensor_tensor(out=dump2_bf[:], in0=xt[:], scalar=1.0, in1=xt[:], op0=ALU.mult, op1=ALU.mult, accum_out=stat[:, 1:2]),
                                waits=[(sSY, xld[sub]), (sSC, c1)] + ([(sT, wqnT0)] if sub == 0 else []), inc=sV,
                                rd=["xa" if sub % 2 == 0 else "xb"], wr=["dump2", "st1"])
                    P.step("vector", lambda v: v.scalar_tensor_tensor(out=stat[:, 2:3], in0=stat[:, 0:1], scalar=1.0 / D, in1=stat[:, 0:1], op0=ALU.mult, op1=ALU.mult), inc=sV, rd=["st1"], wr=["st1"])
                    c5 = P.step("vector", lambda v: v.scalar_tensor_tensor(out=stat[:, 3:4], in0=stat[:, 1:2], scalar=1.0, in1=stat[:, 2:3], op0=ALU.mult, op1=ALU.subtract), inc=sV, rd=["st1"], wr=["st1"])
                    c6 = P.step("scalar", lambda s: s.activation(out=stat[:, 4:5], in_=stat[:, 3:4], func=AF.Sqrt, scale=1.0 / float(D), bias=LN_EPS),
                                waits=[(sV, c5), (sGP, ld_scal)], inc=sSC, rd=["st1", "scal"], wr=["st1"])
                    c7 = P.step("vector", lambda v: v.reciprocal(out=stat[:, 4:5], in_=stat[:, 4:5]),
                                waits=[(sSC, c6)], inc=sV, rd=["st1"], wr=["st1"])
                    c8 = P.step("vector", lambda v: v.scalar_tensor_tensor(out=stat[:, 5:6], in0=stat[:, 0:1], scalar=-1.0 / D, in1=stat[:, 4:5], op0=ALU.mult, op1=ALU.mult),
                                inc=sV, rd=["st1"], wr=["st1"])
                    c9 = P.step("scalar", lambda s, xt=xt: s.activation(out=htok[:], in_=xt[:], func=AF.Identity, scale=stat[:, 4:5], bias=stat[:, 5:6]),
                                waits=[(sV, c8)], inc=sSC, rd=["xa" if sub % 2 == 0 else "xb", "st1"], wr=["htok"])
                    ctx["prev_ln_S"] = c9
                    dbg_gate = []
                    if stop == "p1" and bp == 0 and sub == 0:
                        dxa = P.step("gpsimd", gdma(dbg["d_xa0"], xt[:]), waits=[(sV, c2)], inc=sGP, inc_n=16, rd=["xa"])
                        P.step("sync", sdma(dbg["d_scal"], scal_sb[:]), waits=[(sSC, c6)], inc=sOUT, inc_n=16, rd=["scal"])
                        dst_ = P.step("gpsimd", gdma(dbg["d_st0"], stat[:]), waits=[(sV, c8)], inc=sGP, inc_n=16, rd=["st1"])
                        dxb = P.step("gpsimd", gdma(dbg["d_xa0b"], xt[:]), waits=[(sSC, c9)], inc=sGP, inc_n=16, rd=["xa"])
                        dh0 = P.step("gpsimd", gdma(dbg["d_h0"], htok[:]), waits=[(sSC, c9)], inc=sGP, inc_n=16, rd=["htok"])
                        dbg_gate = [(sGP, dh0)]
                        ctx["p1dbg_gate"] = dh0
                    cA = P.step("vector", lambda v: v.tensor_mul(out=htok[:], in0=htok[:], in1=ca1_rep[:]),
                                waits=[(sSC, c9), (sSC, fr["pWA_S"])] + dbg_gate, inc=sV, rd=["htok", "ca1"], wr=["htok"])
                    cB = P.step("vector", lambda v: v.tensor_add(out=htok[:], in0=htok[:], in1=sa_rep[:]), inc=sV, rd=["htok", "sa"], wr=["htok"])
                    ctx["htok_V"] = cB
                    tv = None
                    for ch in range(KC):
                        tv = P.step("tensor", lambda t, ch=ch: t.transpose(pT[:, ch * 128:(ch + 1) * 128], in_=htok[:, ch * 128:(ch + 1) * 128], identity=ident[:]),
                                    waits=([(sV, cB), (sSC, fr["pT_S"])] if ch == 0 else []),
                                    inc=(sT if ch == KC - 1 else None), rd=["htok", "ident"], wr=["pT"])
                    ctx["htok_T"] = tv
                    hdone[sub] = P.step("scalar", lambda s, sub=sub: s.copy(
                        out=h_fm[:, :, sub * 128:(sub + 1) * 128], in_=pT[:].rearrange("p (c n) -> p c n", n=128)),
                        waits=[(sT, tv)], inc=sSC, rd=["pT"], wr=["h_fm"])
                    fr["pT_S"] = hdone[sub]
                    if sub + 2 < NSUB:
                        xld[sub + 2] = P.step("sync", lambda sy, sub=sub, bp=bp: sy.dma_start(
                            out=xbufs[sub % 2][:], in_=xin[bp][(sub + 2) * 128:(sub + 3) * 128, :]),
                            waits=[(sSC, hdone[sub])], inc=sSY, inc_n=16,
                            wr=["xa" if sub % 2 == 0 else "xb"])
                    pv = pR0 if (sub % 2 == 0) else pR1
                    mmv = None
                    for half in range(2):
                        for ch in range(KC):
                            mmv = P.step("tensor", lambda t, half=half, ch=ch, sub=sub, pv=pv: t.matmul(
                                pv[:, half * 512:(half + 1) * 512],
                                lhsT=h_fm[:, ch, sub * 128:(sub + 1) * 128],
                                rhs=WB[:, ch, half * 512:(half + 1) * 512],
                                start=(ch == 0), stop=(ch == KC - 1)),
                                waits=([(sSC, hdone[sub]), (sGP, ld_cur["v"]),
                                        (sV, vdone[sub - 2] if sub >= 2 else 0),
                                        (sOUT, ctx["pR_OUT"] if sub < 2 else 0)] if (half == 0 and ch == 0) else []),
                                inc=(sT if ch == KC - 1 else None),
                                rd=["h_fm", "WB"], wr=["pR0" if sub % 2 == 0 else "pR1"])
                    last_v_T = mmv
                    vdone[sub] = P.step("vector", lambda v, sub=sub, pv=pv: v.tensor_copy(out=v_tok[:, sub, :], in_=pv[:]),
                                        waits=[(sT, mmv)], inc=sV,
                                        rd=["pR0" if sub % 2 == 0 else "pR1"], wr=["v_tok"])

                kk = None
                kdone_h = {}
                for h in range(H):
                    pk = pW if (h % 2 == 0) else pA
                    for ch in range(KC):
                        kk = P.step("tensor", lambda t, h=h, ch=ch, ii=ii, pk=pk: t.matmul(
                            pk[:, 0:C],
                            lhsT=WA[:, ch, h * 128:(h + 1) * 128],
                            rhs=h_fm[:, ch, ii * C:(ii + 1) * C],
                            start=(ch == 0), stop=(ch == KC - 1)),
                            waits=([(sSC, hdone[2 * ii + 1]), (sGP, ld_cur["k"]),
                                    (sSC, kdone_h.get(h - 2, 0)),
                                    (sV, chain_v if h < 2 else 0), (sSC, fr["pWA_S"] if h < 2 else 0),
                                    (sV, ctx.get("o_ev_prev", 0) if h < 2 else 0),
                                    (sV, fr["pWA_V"] if h < 2 else 0)] if ch == 0 else []),
                            inc=(sT if ch == KC - 1 else None),
                            rd=["h_fm", "WA"], wr=["pW" if h % 2 == 0 else "pA"])
                    kdone_h[h] = P.step("scalar", lambda s, h=h, ii=ii, pk=pk: s.copy(out=k_fm[:, h, ii * C:(ii + 1) * C], in_=pk[:, 0:C]),
                                        waits=[(sT, kk)], inc=sSC,
                                        rd=["pW" if h % 2 == 0 else "pA"], wr=["k_fm"])
                last_k_T = kk
                kev = kdone_h[H - 1]
                ktk_v = 0
                for ms in range(2):
                    tv = None
                    for h in range(H):
                        tv = P.step("tensor", lambda t, h=h, ms=ms, ii=ii: t.transpose(
                            pT[:, h * 128:(h + 1) * 128], in_=k_fm[:, h, ii * C + ms * 128: ii * C + (ms + 1) * 128], identity=ident[:]),
                            waits=([(sSC, kev), (sSC, fr["pT_S"])] if h == 0 else []),
                            inc=(sT if h == H - 1 else None), rd=["k_fm", "ident"], wr=["pT"])
                    kte = None
                    for h in range(H):
                        kte = P.step("scalar", lambda s, ms=ms, h=h: s.activation(
                            out=ktk[:, ms, h * 128:(h + 1) * 128], in_=pT[:, h * 128:(h + 1) * 128],
                            func=AF.Identity, scale=dkcol[:, ms * 8 + h:ms * 8 + h + 1]),
                            waits=([(sT, tv), (sGP, ld_dk)] if h == 0 else []), inc=sSC,
                            rd=["pT", "dkc"], wr=["ktk%d" % ms])
                    fr["pT_S"] = kte
                    ktk_v = kte
                st_mm = None
                for h in range(H):
                    ps_ = pW if h < 4 else pA
                    col = (h % 4) * 128
                    for ms in range(2):
                        st_mm = P.step("tensor", lambda t, h=h, ms=ms, ii=ii, ps_=ps_, col=col: t.matmul(
                            ps_[:, col:col + 128],
                            lhsT=ktk[:, ms, h * 128:(h + 1) * 128],
                            rhs=v_tok[:, 2 * ii + ms, h * 128:(h + 1) * 128],
                            start=(ms == 0), stop=(ms == 1)),
                            waits=([(sSC, ktk_v), (sV, vdone[2 * ii + 1]), (sSC, kev)] if (h == 0 and ms == 0) else []),
                            inc=(sT if ms == 1 else None), rd=["ktk%d" % ms, "v_tok"], wr=["pW" if h < 4 else "pA"])
                P.step("vector", lambda v, ii=ii: v.tensor_copy(out=o_save[:, ii, 0:512], in_=pW[:, 0:512]),
                       waits=[(sT, st_mm)] + ([(sV, ctx.get("osave_read_V", 0))] if ii == 0 else []),
                       inc=sV, rd=["pW"], wr=["osave"])
                o_ev = P.step("vector", lambda v, ii=ii: v.tensor_copy(out=o_save[:, ii, 512:1024], in_=pA[:, 0:512]),
                       inc=sV, rd=["pA"], wr=["osave"])
                ctx["o_ev_prev"] = o_ev
                for h in range(H):
                    ps_ = pW if h < 4 else pA
                    col = (h % 4) * 128
                    chain_v = P.step("vector", lambda v, h=h, ps_=ps_, col=col: v.scalar_tensor_tensor(
                        out=s_f32[:, h * 128:(h + 1) * 128], in0=s_f32[:, h * 128:(h + 1) * 128],
                        scalar=gam_c[h], in1=ps_[:, col:col + 128], op0=ALU.mult, op1=ALU.add),
                        waits=([(sT, st_mm)] if h == 0 else []), inc=sV,
                        rd=["sf%d" % h, "pW" if h < 4 else "pA"], wr=["sf%d" % h])
                fr["pWA_V"] = chain_v

            ctx["last_k_T"] = last_k_T
            ctx["last_v_T"] = last_v_T
            ctx["ktk_free_T"] = st_mm
            ctx["ktk_free_S"] = ktk_v
            ctx["osave_last_V"] = o_ev
            ld_cur["g"] = P.step("gpsimd", gdma(WB[:], w_in["g"].rearrange("(c p) n -> p c n", p=128)),
                                 waits=[(sT, last_v_T)], inc=sGP, inc_n=16)
            def g_eff(dst, grp):
                mdf = load_group(grp, bp, want_bf=False)
                a = P.step("vector", lambda v: v.tensor_single_scalar(out=grow, in_=modg, scalar=0.0, op=ALU.is_ge),
                           waits=[(sGP, mdf)], inc=sV, rd=["xb"], wr=["wfg"])
                P.step("vector", lambda v: v.tensor_scalar(out=grow, in0=grow, scalar1=2.0, scalar2=-1.0, op0=ALU.mult, op1=ALU.add), inc=sV, rd=["wfg"], wr=["wfg"])
                c_ = P.step("scalar", lambda s: s.activation(out=grow2, in_=modg, func=AF.Abs),
                            waits=[(sGP, mdf)], inc=sSC, rd=["xb"], wr=["xa"])
                P.step("vector", lambda v: v.tensor_scalar_max(out=grow2, in0=grow2, scalar1=0.25), waits=[(sSC, c_)], inc=sV, rd=["xa"], wr=["xa"])
                e_ = P.step("vector", lambda v, dst=dst: v.tensor_mul(out=dst[0:1, :], in0=grow, in1=grow2), inc=sV,
                            rd=["wfg", "xa"], wr=["wf1" if dst is gb_row else "wfg"])
                ctx["modg_V"] = e_
                ctx["modg_S"] = c_
                fr["wfst_V"] = max(fr.get("wfst_V", 0), e_)
                fr["wfst_S"] = max(fr.get("wfst_S", 0), c_)
                return e_

            gb_done = g_eff(gb_row, 5)
            if stop == "p2":
                P.step("gpsimd", gdma(dbg["d_sign"][0:1, :], gb_row), waits=[(sV, gb_done)], inc=sGP, inc_n=16)
            ga_done = g_eff(grow, 2)
            if stop == "p2":
                P.step("gpsimd", gdma(dbg["d_sign"][1:2, :], grow), waits=[(sV, ga_done)], inc=sGP, inc_n=16)

            ld_cur["o"] = P.step("gpsimd", gdma(WA[:], w_in["o"].rearrange("(c p) n -> p c n", p=128)),
                                 waits=[(sT, last_k_T)], inc=sGP, inc_n=16, wr=["WA"])
            gae = None
            for hnf in range(2):
                tv = P.step("tensor", lambda t, hnf=hnf: t.matmul(
                    pB[:, 0:512], lhsT=ones_row[0:1, :],
                    rhs=grow[0:1, hnf * 512:(hnf + 1) * 512], start=True, stop=True),
                    waits=[(sV, ga_done), (sSC, fr["pB_S"])], inc=sT,
                    rd=["wfg", "ones"], wr=["pB"])
                gae = P.step("scalar", lambda s, hnf=hnf: s.copy(
                    out=g_rep[:, hnf * 512:(hnf + 1) * 512], in_=pB[:, 0:512]),
                    waits=[(sT, tv)], inc=sSC, rd=["pB"], wr=["aqs"])
                fr["pB_S"] = gae
            wo_fold = 0
            for ch in range(KC):
                wo_fold = P.step("vector", lambda v, ch=ch: v.tensor_mul(out=WA[:, ch, :], in0=WA[:, ch, :], in1=g_rep[:]),
                                 waits=([(sGP, ld_cur["o"]), (sSC, gae)] if ch == 0 else []), inc=sV,
                                 rd=["WA", "aqs"], wr=["WA"])
            gbe = None
            for hnf in range(2):
                tv = P.step("tensor", lambda t, hnf=hnf: t.matmul(
                    pB[:, 0:512], lhsT=ones_row[0:1, :],
                    rhs=gb_row[0:1, hnf * 512:(hnf + 1) * 512], start=True, stop=True),
                    waits=[(sV, gb_done), (sSC, fr["pB_S"]), (sV, wo_fold)], inc=sT, rd=["wf1", "ones"], wr=["pB"])
                gbe = P.step("scalar", lambda s, hnf=hnf: s.copy(
                    out=g_rep[:, hnf * 512:(hnf + 1) * 512], in_=pB[:, 0:512]),
                    waits=[(sT, tv)], inc=sSC, rd=["pB"], wr=["aqs"])
                fr["pB_S"] = gbe
            if stop == "p2":
                P.step("gpsimd", gdma(dbg["d_sign"][2:3, :], g_rep[0:1, :]), waits=[(sSC, gbe)], inc=sGP, inc_n=16)
            wq_fold = 0
            for ch in range(KC):
                wq_fold = P.step("vector", lambda v, ch=ch: v.tensor_mul(out=wqt[:, ch, :], in0=wqt[:, ch, :], in1=g_rep[:]),
                                 waits=([(sSC, gbe), (sSC, wqt_done)] if ch == 0 else []), inc=sV,
                                 rd=["wqt", "aqs"], wr=["wqt"])


            sd1 = P.step("gpsimd", gdma(cc_st_in[bp].ap(), s_f32[:]), waits=[(sV, chain_v)], inc=sGP, inc_n=16,
                         rd=SF_ALL, wr=["ccsi%d" % bp])
            cc2 = P.step("gpsimd", lambda g, bp=bp: g.collective_compute(
                "AllGather", ALU.bypass, replica_groups=[list(range(NCORES))],
                ins=[cc_st_in[bp].ap()], outs=[cc_st_out[bp].ap()]),
                waits=[(sGP, sd1)], inc=sCC, rd=["ccsi%d" % bp], wr=["ccso%d" % bp])

            if not run_p2:
                return cc2


            x2ld = [None] * NSUB
            for sub in range(2):
                x2ld[sub] = P.step("gpsimd", lambda g, sub=sub, bp=bp: g.dma_start(
                    out=xbf[:, sub % 2, :], in_=xin[bp][sub * 128:(sub + 1) * 128, :]),
                    waits=[(sT, ctx["xbf_T"])], inc=sGP, inc_n=16, wr=["xbf%d" % (sub % 2)])

            ffn_done = [0, 0]
            h2s = {"qt_ev": 0, "g_ev": 0, "y_free": 0, "out_ev": 0}

            def emit_qproj(ii):
                for hp in range(4):
                    pq = pW if (hp % 2 == 0) else pA
                    qq = None
                    for h2_ in range(2):
                        h = hp * 2 + h2_
                        for ch in range(KC):
                            qq = P.step("tensor", lambda t, h=h, ch=ch, ii=ii, pq=pq, h2_=h2_: t.matmul(
                                pq[:, h2_ * C:(h2_ + 1) * C],
                                lhsT=WC[:, ch, h * 128:(h + 1) * 128],
                                rhs=h_fm[:, ch, ii * C:(ii + 1) * C],
                                start=(ch == 0), stop=(ch == KC - 1)),
                                waits=([(sGP, ld_cur["q"]),
                                        (sV, h2s["qt_ev"] if hp >= 2 else fr["pWA_V"]),
                                        (sT, ctx["ret_all_prev"] if hp < 2 else 0)] if (h2_ == 0 and ch == 0) else []),
                                inc=(sT if ch == KC - 1 else None),
                                rd=["h_fm", "WC"], wr=["pW" if hp % 2 == 0 else "pA"])
                    h2s["qt_ev"] = P.step("vector", lambda v, hp=hp, pq=pq: v.tensor_tensor(
                        out=qt_ch[:, hp * 2:(hp + 1) * 2, :].rearrange("p h n -> p (h n)"), in0=pq[:, 0:512],
                        in1=lam[:, hp * 2:(hp + 1) * 2, :].rearrange("p h n -> p (h n)"), op=ALU.mult),
                        waits=[(sT, qq), (sGP, ld_lam)], inc=sV,
                        rd=["pW" if hp % 2 == 0 else "pA", "lam"], wr=["qt"])
                fr["pWA_V"] = h2s["qt_ev"]

            def emit_G(ii):
                for ns in range(2):
                    pg = pR0 if ns == 0 else pR1
                    gg = None
                    for half in range(2):
                        for ch in range(KC):
                            gg = P.step("tensor", lambda t, ns=ns, half=half, ch=ch, ii=ii, pg=pg: t.matmul(
                                pg[:, half * 512:(half + 1) * 512],
                                lhsT=h_fm[:, ch, ii * C + ns * 128: ii * C + (ns + 1) * 128],
                                rhs=WB[:, ch, half * 512:(half + 1) * 512],
                                start=(ch == 0), stop=(ch == KC - 1)),
                                waits=([(sGP, ld_cur["g"]), (sOUT, ffn_done[ns]),
                                        (sV, vdone[NSUB - 2 + ns] if ii == 0 else 0),
                                        (sV, h2s["y_free"])] if (half == 0 and ch == 0) else []),
                                inc=(sT if ch == KC - 1 else None),
                                rd=["h_fm", "WB"], wr=["pR0" if ns == 0 else "pR1"])
                    ctx["last_G_T"] = gg
                    h2s["g_ev"] = P.step("scalar", lambda s, ns=ns, pg=pg: s.activation(out=g_ch[:, ns, :], in_=pg[:], func=AF.Silu),
                                  waits=[(sT, gg), (sV, h2s["y_free"])], inc=sSC,
                                  rd=["pR0" if ns == 0 else "pR1"], wr=["gch"])

            emit_qproj(0)
            emit_G(0)
            zv2 = P.step("vector", lambda v: v.memset(s_f32[:], 0.0), waits=[(sGP, sd1)], inc=sV, wr=SF_ALL)
            comb = zv2
            stjv = 0
            for j in range(NCORES):
                stjv = P.step("gpsimd", lambda g, j=j, bp=bp: g.dma_start(
                    out=sbf_all[:, NCH - 1, :], in_=cc_st_out[bp].ap()[j * 128:(j + 1) * 128, :]),
                    waits=[(sCC, cc2), (sGP, ld_cfac), (sV, comb)] +
                          ([(sT, ctx["ret_all_prev"])] if j == 0 else []), inc=sGP, inc_n=16,
                    rd=["ccso%d" % bp], wr=["sbfall"])
                for h in range(H):
                    comb = P.step("vector", lambda v, j=j, h=h: v.scalar_tensor_tensor(
                        out=s_f32[:, h * 128:(h + 1) * 128], in0=sbf_all[:, NCH - 1, h * 128:(h + 1) * 128],
                        scalar=cfac[:, j * 8 + h:j * 8 + h + 1], in1=s_f32[:, h * 128:(h + 1) * 128],
                        op0=ALU.mult, op1=ALU.add),
                        waits=([(sGP, stjv)] if h == 0 else []), inc=sV,
                        rd=["sbfall", "cfac", "sf%d" % h], wr=["sf%d" % h])
            sbf_ready = [0] * NCH
            sbf_ready[0] = P.step("vector", lambda v: v.tensor_copy(out=sbf_all[:, 0, :], in_=s_f32[:]),
                                  waits=[(sV, comb), (sT, ctx["ret_all_prev"])], inc=sV, rd=SF_ALL, wr=["sbfall"])
            chx = comb
            for i2 in range(1, NCH):
                for h in range(H):
                    chx = P.step("vector", lambda v, i2=i2, h=h: v.scalar_tensor_tensor(
                        out=s_f32[:, h * 128:(h + 1) * 128], in0=s_f32[:, h * 128:(h + 1) * 128],
                        scalar=gam_c[h], in1=o_save[:, i2 - 1, h * 128:(h + 1) * 128],
                        op0=ALU.mult, op1=ALU.add),
                        waits=([(sV, sbf_ready[i2 - 1]), (sV, ctx["o_ev_prev"])] if h == 0 else []), inc=sV,
                        rd=["sf%d" % h, "osave"], wr=["sf%d" % h])
                sbf_ready[i2] = P.step("vector", lambda v, i2=i2: v.tensor_copy(out=sbf_all[:, i2, :], in_=s_f32[:]),
                                       waits=[(sV, chx)], inc=sV, rd=SF_ALL, wr=["sbfall"])
            ctx["osave_read_V"] = chx
            ctx["sbf_used"] = sbf_ready[NCH - 1]

            for ii in range(NCH):
                if ii > 0:
                    emit_G(ii)
                qt_ev = h2s["qt_ev"]
                g_ev = h2s["g_ev"]
                out_ev = h2s["out_ev"]
                if stop == "p2" and bp == 0:
                    P.step("gpsimd", gdma(dbg["d_qt"].rearrange("p h t -> p h t")[:, :, ii * C:(ii + 1) * C], qt_ch[:]),
                           waits=[(sV, qt_ev)], inc=sGP, inc_n=16)
                    P.step("gpsimd", gdma(dbg["d_sbf"][ii], sbf_all[:, ii, :]), waits=[(sV, sbf_ready[ii])], inc=sGP, inc_n=16)
                rets = {}
                at_ev = {}
                for h in range(H):
                    psc = pW if (h % 2 == 0) else pA
                    P.step("tensor", lambda t, h=h, ii=ii, psc=psc: t.matmul(
                        psc[:, 0:C], lhsT=k_fm[:, h, ii * C:ii * C + 128], rhs=qt_ch[:, h, :],
                        start=True, stop=True),
                        waits=[(sV, qt_ev), (sV, at_ev.get(h - 2, 0))], inc=sT,
                        rd=["k_fm", "qt"], wr=["pW" if h % 2 == 0 else "pA"])
                    s2_ = P.step("tensor", lambda t, h=h, ii=ii, psc=psc: t.matmul(
                        psc[:, C:C + 128], lhsT=k_fm[:, h, ii * C + 128:ii * C + 256], rhs=qt_ch[:, h, 128:256],
                        start=True, stop=True), inc=sT,
                        rd=["k_fm", "qt"], wr=["pW" if h % 2 == 0 else "pA"])
                    at_ev[h] = P.step("vector", lambda v, h=h, psc=psc: v.tensor_tensor(
                        out=at_bf[:, h % 2, :], in0=psc[:, 0:384], in1=dtp[:, h, :], op=ALU.mult),
                        waits=[(sT, s2_), (sGP, ld_dtp), (sT, rets.get((h - 2, 1), 0))], inc=sV,
                        rd=["pW" if h % 2 == 0 else "pA", "dtp"], wr=["at%d" % (h % 2)])
                    for ns in range(2):
                        pr = pR0 if ns == 0 else pR1
                        blocks = [0] if ns == 0 else [0, 1]
                        first = True
                        for msb in blocks:
                            lh = (ns * 128 if msb == 0 else 256)
                            P.step("tensor", lambda t, h=h, ii=ii, pr=pr, msb=msb, lh=lh, first=first: t.matmul(
                                pr[:, h * 128:(h + 1) * 128],
                                lhsT=at_bf[:, h % 2, lh:lh + 128],
                                rhs=v_tok[:, 2 * ii + msb, h * 128:(h + 1) * 128],
                                start=first, stop=False),
                                waits=([(sV, at_ev[h]), (sV, sbf_ready[ii]), (sSC, g_ev)] if first else []),
                                inc=None, rd=["at%d" % (h % 2), "v_tok"], wr=["pR0" if ns == 0 else "pR1"])
                            first = False
                        rets[(h, ns)] = P.step("tensor", lambda t, h=h, ns=ns, ii=ii, pr=pr: t.matmul(
                            pr[:, h * 128:(h + 1) * 128],
                            lhsT=qt_ch[:, h, ns * 128:(ns + 1) * 128], rhs=sbf_all[:, ii, h * 128:(h + 1) * 128],
                            start=False, stop=True), inc=sT,
                            rd=["qt", "sbfall"], wr=["pR0" if ns == 0 else "pR1"])
                ret_all = rets[(H - 1, 1)]
                ctx["ret_all_prev"] = ret_all
                fr["pWA_V"] = at_ev[H - 1]
                if ii + 1 < NCH:
                    emit_qproj(ii + 1)

                gn_ev = 0
                for ns in range(2):
                    pr = pR0 if ns == 0 else pR1
                    for h in range(H):
                        gn_ev = P.step("scalar", lambda s, ns=ns, h=h, pr=pr: s.activation(
                            out=y_raw[:, ns, h * 128:(h + 1) * 128], in_=pr[:, h * 128:(h + 1) * 128],
                            func=AF.Copy, accum_out=gstat[:, ns * 8 + h:ns * 8 + h + 1]),
                            waits=[(sT, rets[(h, ns)])] + ([(sV, h2s["y_free"])] if (ns == 0 and h == 0) else []), inc=sSC,
                            rd=["pR0" if ns == 0 else "pR1"], wr=["y", "gstat"])
                if stop == "p2" and bp == 0:
                    for ns_ in range(2):
                        P.step("gpsimd", gdma(dbg["d_rr"][(2 * ii + ns_) * 128:(2 * ii + ns_ + 1) * 128, :], y_raw[:, ns_, :]),
                               waits=[(sSC, gn_ev)], inc=sGP, inc_n=16)
                sq_v = 0
                for ns in range(2):
                    for h in range(H):
                        sq_v = P.step("vector", lambda v, ns=ns, h=h: v.scalar_tensor_tensor(
                            out=dump2_bf[:, 0:128], in0=y_raw[:, ns, h * 128:(h + 1) * 128], scalar=1.0,
                            in1=y_raw[:, ns, h * 128:(h + 1) * 128], op0=ALU.mult, op1=ALU.mult,
                            accum_out=gstat[:, 16 + ns * 8 + h:17 + ns * 8 + h]),
                            waits=[(sSC, gn_ev)], inc=sV, rd=["y"], wr=["dump2", "gstat"])
                P.step("vector", lambda v: v.scalar_tensor_tensor(out=gstat[:, 32:48], in0=gstat[:, 0:16], scalar=1.0 / DH, in1=gstat[:, 0:16], op0=ALU.mult, op1=ALU.mult),
                       waits=[(sV, sq_v)], inc=sV, rd=["gstat"], wr=["gstat"])
                g3 = P.step("vector", lambda v: v.scalar_tensor_tensor(out=gstat[:, 48:64], in0=gstat[:, 16:32], scalar=1.0, in1=gstat[:, 32:48], op0=ALU.mult, op1=ALU.subtract), inc=sV, rd=["gstat"], wr=["gstat"])
                g4 = P.step("scalar", lambda s: s.activation(out=gstat[:, 32:48], in_=gstat[:, 48:64], func=AF.Sqrt, scale=1.0 / float(DH), bias=GN_EPS),
                            waits=[(sV, g3), (sGP, ld_scal)], inc=sSC, rd=["gstat", "scal"], wr=["gstat"])
                g5 = P.step("vector", lambda v: v.reciprocal(out=gstat[:, 32:48], in_=gstat[:, 32:48]),
                            waits=[(sSC, g4)], inc=sV, rd=["gstat"], wr=["gstat"])
                g6 = P.step("vector", lambda v: v.scalar_tensor_tensor(out=gstat[:, 48:64], in0=gstat[:, 0:16], scalar=-1.0 / DH, in1=gstat[:, 32:48], op0=ALU.mult, op1=ALU.mult),
                            inc=sV, rd=["gstat"], wr=["gstat"])
                if stop == "p2" and bp == 0:
                    P.step("gpsimd", gdma(dbg["d_gst"][ii], gstat[:]), waits=[(sV, g6)], inc=sGP, inc_n=16)
                yz_v = 0
                yz_ns = {}
                for ns in range(2):
                    norm_s = 0
                    for h in range(H):
                        if ns == 0:
                            P.step("vector", lambda v, ns=ns, h=h: v.tensor_scalar(
                                out=y_raw[:, ns, h * 128:(h + 1) * 128], in0=y_raw[:, ns, h * 128:(h + 1) * 128],
                                scalar1=gstat[:, 32 + ns * 8 + h:33 + ns * 8 + h], scalar2=gstat[:, 48 + ns * 8 + h:49 + ns * 8 + h],
                                op0=ALU.mult, op1=ALU.add),
                                waits=([(sV, g6)] if h == 0 else []), inc=sV, rd=["y", "gstat"], wr=["y"])
                        else:
                            norm_s = P.step("scalar", lambda s, ns=ns, h=h: s.activation(
                                out=y_raw[:, ns, h * 128:(h + 1) * 128], in_=y_raw[:, ns, h * 128:(h + 1) * 128],
                                func=AF.Identity, scale=gstat[:, 32 + ns * 8 + h:33 + ns * 8 + h],
                                bias=gstat[:, 48 + ns * 8 + h:49 + ns * 8 + h]),
                                waits=([(sV, g6)] if h == 0 else []), inc=sSC, rd=["y", "gstat"], wr=["y"])
                    if stop == "p2" and bp == 0:
                        P.step("gpsimd", gdma(dbg["d_y"][(2 * ii + ns) * 128:(2 * ii + ns + 1) * 128, :], y_raw[:, ns, :]),
                               waits=[(sV, sV.n)], inc=sGP, inc_n=16)
                        P.step("gpsimd", gdma(dbg["d_g"][(2 * ii + ns) * 128:(2 * ii + ns + 1) * 128, :], g_ch[:, ns, :]),
                               waits=[(sSC, g_ev)], inc=sGP, inc_n=16)
                    yz_v = P.step("vector", lambda v, ns=ns: v.tensor_mul(out=y_raw[:, ns, :], in0=y_raw[:, ns, :], in1=g_ch[:, ns, :]),
                                  waits=[(sSC, g_ev), (sSC, norm_s)], inc=sV, rd=["y", "gch"], wr=["y"])
                    yz_ns[ns] = yz_v
                h2s["y_free"] = yz_v

                mo_ns = {}
                for ns in range(2):
                    pr = pR0 if ns == 0 else pR1
                    sub = 2 * ii + ns
                    tv = None
                    for ch in range(KC):
                        tv = P.step("tensor", lambda t, ns=ns, ch=ch: t.transpose(
                            pT[:, ch * 128:(ch + 1) * 128], in_=y_raw[:, ns, ch * 128:(ch + 1) * 128], identity=ident[:]),
                            waits=([(sV, yz_ns[ns]), (sSC, fr["pT_S"]), (sT, ctx["ffn_T_ns"][ns])] if ch == 0 else []),
                            inc=(sT if ch == KC - 1 else None), rd=["y", "ident"], wr=["pT"])
                    zfm_s = P.step("scalar", lambda s, ns=ns: s.copy(
                        out=zfm[:, :, ns * 128:(ns + 1) * 128], in_=pT[:].rearrange("p (c n) -> p c n", n=128)),
                        waits=[(sT, tv)], inc=sSC, rd=["pT"], wr=["zfm"])
                    fr["pT_S"] = zfm_s
                    mo = None
                    for half in range(2):
                        for ch in range(KC):
                            mo = P.step("tensor", lambda t, ns=ns, half=half, ch=ch, pr=pr: t.matmul(
                                pr[:, half * 512:(half + 1) * 512],
                                lhsT=zfm[:, ch, ns * 128:(ns + 1) * 128],
                                rhs=WA[:, ch, half * 512:(half + 1) * 512],
                                start=(ch == 0), stop=False),
                                waits=([(sSC, zfm_s), (sV, wo_fold), (sSC, gn_ev), (sV, sq_v)] if (half == 0 and ch == 0) else []),
                                inc=None, rd=["zfm", "WA"], wr=["pR0" if ns == 0 else "pR1"])
                        mo = P.step("tensor", lambda t, ns=ns, half=half, pr=pr, sub=sub: t.matmul(
                            pr[:, half * 512:(half + 1) * 512],
                            lhsT=ident[:], rhs=xbf[:, sub % 2, half * 512:(half + 1) * 512],
                            start=False, stop=True),
                            waits=[(sGP, x2ld[sub])], inc=sT, rd=["ident", "xbf%d" % (sub % 2)], wr=["pR0" if ns == 0 else "pR1"])
                    mo_ns[ns] = mo
                    ctx["last_oproj_T"] = mo
                    ctx["xbf_T"] = mo
                    if sub + 2 < NSUB:
                        x2ld[sub + 2] = P.step("gpsimd", lambda g, sub=sub, bp=bp: g.dma_start(
                            out=xbf[:, sub % 2, :], in_=xin[bp][(sub + 2) * 128:(sub + 3) * 128, :]),
                            waits=[(sT, mo)], inc=sGP, inc_n=16, wr=["xbf%d" % (sub % 2)])
                for ns in range(2):
                    pr = pR0 if ns == 0 else pR1
                    sub = 2 * ii + ns
                    mo = mo_ns[ns]
                    xn = P.step("scalar", lambda s, ns=ns, pr=pr: s.activation(
                        out=xnew[:], in_=pr[:], func=AF.Copy, accum_out=stat[:, 8 + ns:9 + ns]),
                        waits=[(sT, mo), (sT, ctx["last_ffn_T"]), (sV, ctx.get("xnew_V", 0))], inc=sSC,
                        rd=["pR0" if ns == 0 else "pR1"], wr=["xnew", "st2"])
                    if stop == "p2" and bp == 0:
                        P.step("gpsimd", gdma(dbg["d_xnew"][sub * 128:(sub + 1) * 128, :], xnew[:]),
                               waits=[(sSC, xn)], inc=sGP, inc_n=16)
                    s2v = P.step("vector", lambda v, ns=ns: v.scalar_tensor_tensor(
                        out=dump2_bf[:], in0=xnew[:], scalar=1.0, in1=xnew[:], op0=ALU.mult, op1=ALU.mult,
                        accum_out=stat[:, 10 + ns:11 + ns]),
                        waits=[(sSC, xn)], inc=sV, rd=["xnew"], wr=["dump2", "st2"])
                    ctx["xnew_V"] = s2v
                    P.step("vector", lambda v, ns=ns: v.scalar_tensor_tensor(out=stat[:, 12:13], in0=stat[:, 8 + ns:9 + ns], scalar=1.0 / D, in1=stat[:, 8 + ns:9 + ns], op0=ALU.mult, op1=ALU.mult), inc=sV, rd=["st2"], wr=["st2"])
                    l3 = P.step("vector", lambda v, ns=ns: v.scalar_tensor_tensor(out=stat[:, 13:14], in0=stat[:, 10 + ns:11 + ns], scalar=1.0, in1=stat[:, 12:13], op0=ALU.mult, op1=ALU.subtract),
                                waits=[(sV, s2v)], inc=sV, rd=["st2"], wr=["st2"])
                    l4 = P.step("scalar", lambda s: s.activation(out=stat[:, 14:15], in_=stat[:, 13:14], func=AF.Sqrt, scale=1.0 / float(D), bias=LN_EPS), waits=[(sV, l3), (sGP, ld_scal)], inc=sSC, rd=["st2", "scal"], wr=["st2"])
                    l5 = P.step("vector", lambda v: v.reciprocal(out=stat[:, 14:15], in_=stat[:, 14:15]),
                                waits=[(sSC, l4)], inc=sV, rd=["st2"], wr=["st2"])
                    l6 = P.step("vector", lambda v, ns=ns: v.scalar_tensor_tensor(out=stat[:, 15:16], in0=stat[:, 8 + ns:9 + ns], scalar=-1.0 / D, in1=stat[:, 14:15], op0=ALU.mult, op1=ALU.mult),
                                inc=sV, rd=["st2"], wr=["st2"])
                    h21 = P.step("scalar", lambda s, ns=ns: s.activation(out=h2c[:], in_=xnew[:], func=AF.Identity, scale=stat[:, 14:15], bias=stat[:, 15:16]),
                                 waits=[(sV, l6)], inc=sSC, rd=["xnew", "st2"], wr=["h2c"])
                    P.step("vector", lambda v, ns=ns: v.tensor_mul(out=h2c[:], in0=h2c[:], in1=cb1_rep[:]), waits=[(sSC, h21)], inc=sV, rd=["h2c", "cb1"], wr=["h2c"])
                    h23 = P.step("vector", lambda v, ns=ns: v.tensor_add(out=h2c[:], in0=h2c[:], in1=sb_rep[:]), inc=sV, rd=["h2c", "sb"], wr=["h2c"])
                    am = P.step("vector", lambda v, ns=ns: v.reduce_max(out=stat[:, 12:13], in_=h2c[:], axis=AX.X, apply_absolute_value=True),
                                waits=[(sV, h23)], inc=sV, rd=["h2c"], wr=["st2"])
                    P.step("vector", lambda v: v.tensor_scalar_add(out=stat[:, 13:14], in0=stat[:, 12:13], scalar1=FFN_EPS), inc=sV, rd=["st2"], wr=["st2"])
                    P.step("vector", lambda v: v.reciprocal(out=stat[:, 13:14], in_=stat[:, 13:14]), inc=sV, rd=["st2"], wr=["st2"])
                    q4 = P.step("vector", lambda v: v.tensor_scalar_mul(out=stat[:, 13:14], in0=stat[:, 13:14], scalar1=127.0), inc=sV, rd=["st2"], wr=["st2"])
                    q6 = P.step("vector", lambda v: v.scalar_tensor_tensor(out=stat[:, 15:16], in0=stat[:, 12:13], scalar=FFN_EPS, in1=wsc_c[:, 0:1], op0=ALU.add, op1=ALU.mult),
                                waits=[(sSC, fr["pB_S"])], inc=sV, rd=["st2", "wsc"], wr=["st2"])
                    z1 = P.step("scalar", lambda s, ns=ns: s.activation(out=wmag, in_=h2c[:], func=AF.Identity, scale=stat[:, 13:14]),
                                waits=[(sV, q4)], inc=sSC, rd=["h2c", "st2"], wr=["wf1"])
                    z2 = P.step("scalar", lambda s, ns=ns: s.activation(out=sgn[:], in_=h2c[:], func=AF.Sign), inc=sSC, rd=["h2c"], wr=["sgn"])
                    z3 = P.step("vector", lambda v: v.scalar_tensor_tensor(out=wmag, in0=sgn[:], scalar=128.0, in1=wmag, op0=ALU.mult, op1=ALU.add),
                                waits=[(sSC, z2)], inc=sV, rd=["sgn", "wf1"], wr=["wf1"])
                    z4 = P.step("scalar", lambda s: s.copy(out=wb16s[:], in_=wmag), waits=[(sV, z3)], inc=sSC, rd=["wf1"], wr=["wb16"])
                    z5 = P.step("vector", lambda v: v.scalar_tensor_tensor(out=wmag, in0=sgn[:], scalar=-128.0, in1=wb16s[:], op0=ALU.mult, op1=ALU.add),
                                waits=[(sSC, z4)], inc=sV, rd=["sgn", "wb16"], wr=["wf1"])
                    if stop == "p2" and bp == 0:
                        P.step("gpsimd", gdma(dbg["d_h2"][sub * 128:(sub + 1) * 128, :], h2c[:]),
                               waits=[(sV, sV.n)], inc=sGP, inc_n=16)
                    aq_v = P.step("vector", lambda v, ns=ns: v.tensor_scalar(out=aqs[:], in0=wmag, scalar1=stat[:, 15:16], scalar2=None, op0=ALU.mult),
                                  waits=[(sV, q6), (sV, z5)], inc=sV, rd=["wf1", "st2"], wr=["aqs"])
                    fr["wfst_V"] = max(fr.get("wfst_V", 0), aq_v)
                    fr["wfst_S"] = max(fr.get("wfst_S", 0), z4)
                    if stop == "p2" and bp == 0:
                        P.step("gpsimd", gdma(dbg["d_aq"][sub * 128:(sub + 1) * 128, :], aqs[:]),
                               waits=[(sV, aq_v)], inc=sGP, inc_n=16)
                    tv = None
                    for ch in range(KC):
                        tv = P.step("tensor", lambda t, ns=ns, ch=ch: t.transpose(
                            pT[:, ch * 128:(ch + 1) * 128], in_=aqs[:, ch * 128:(ch + 1) * 128], identity=ident[:]),
                            waits=([(sV, aq_v), (sSC, fr["pT_S"])] if ch == 0 else []),
                            inc=(sT if ch == KC - 1 else None), rd=["aqs", "ident"], wr=["pT"])
                    afm_s = P.step("scalar", lambda s, ns=ns: s.copy(
                        out=zfm[:, :, ns * 128:(ns + 1) * 128], in_=pT[:].rearrange("p (c n) -> p c n", n=128)),
                        waits=[(sT, tv)], inc=sSC, rd=["pT"], wr=["zfm"])
                    fr["pT_S"] = afm_s
                    mf = None
                    for half in range(2):
                        for ch in range(KC):
                            mf = P.step("tensor", lambda t, ns=ns, half=half, ch=ch, pr=pr: t.matmul(
                                pr[:, half * 512:(half + 1) * 512],
                                lhsT=zfm[:, ch, ns * 128:(ns + 1) * 128],
                                rhs=wqt[:, ch, half * 512:(half + 1) * 512],
                                start=(ch == 0), stop=False),
                                waits=([(sSC, afm_s), (sV, wq_fold), (sSC, xn), (sV, s2v)] if (half == 0 and ch == 0) else []),
                                inc=None, rd=["zfm", "wqt"], wr=["pR0" if ns == 0 else "pR1"])
                        mf = P.step("tensor", lambda t, ns=ns, half=half, pr=pr: t.matmul(
                            pr[:, half * 512:(half + 1) * 512],
                            lhsT=ident[:], rhs=xnew[:, half * 512:(half + 1) * 512],
                            start=False, stop=True), inc=sT, rd=["ident", "xnew"], wr=["pR0" if ns == 0 else "pR1"])
                    ctx["last_ffn_T"] = mf
                    ctx["ffn_T_ns"][ns] = mf
                    oev = P.step("vector", lambda v, ns=ns, pr=pr: v.tensor_copy(out=wfst[:, 0, :], in_=pr[:]),
                                 waits=[(sT, mf), (sOUT, out_ev), (sSC, gae)], inc=sV,
                                 rd=["pR0" if ns == 0 else "pR1"], wr=["wf0d", "wfg"])
                    fr["wfst_V"] = max(fr.get("wfst_V", 0), oev)
                    out_ev = P.step("sync", lambda sy, sub=sub, bp=bp: sy.dma_start(
                        out=outd[bp][sub * 128:(sub + 1) * 128, :], in_=wfst[:, 0, :]),
                        waits=[(sV, oev)], inc=sOUT, inc_n=16, rd=["wf0d", "wfg"])
                    ffn_done[ns] = out_ev
                    h2s["out_ev"] = out_ev
            ctx["pR_OUT"] = ffn_done[1]
            return cc2

        if stop == "p1":
            cc2 = phase(0, run_p2=False)
            dv = P.step("vector", lambda v: v.tensor_copy(out=wmag, in_=s_f32[:]), waits=[(sV, sV.n)], inc=sV)
            P.step("sync", sdma(dbg["d_sf32"], wmag), waits=[(sV, dv)], inc=sOUT, inc_n=16)
            for nm, t_ in [("d_hfm", h_fm), ("d_kfm", k_fm), ("d_vtok", v_tok)]:
                P.step("gpsimd", gdma(dbg[nm], t_[:]), waits=[(sV, dv)], inc=sGP, inc_n=16)
            P.step("gpsimd", gdma(dbg["d_gath"].rearrange("a b -> a b"), cc_st_out[0].ap()), waits=[(sCC, cc2)], inc=sGP, inc_n=16)
            for b_ in range(B):
                for grp in range(6):
                    st = (grp * D) // WSLICE
                    en = ((grp + 1) * D - 1) // WSLICE
                    for sl in range(st, en + 1):
                        a = max(grp * D, sl * WSLICE)
                        b2 = min((grp + 1) * D, (sl + 1) * WSLICE)
                        P.step("sync", sdma(dbg["d_mod"][b_:b_ + 1, a:b2],
                                            cc_mod_out.ap()[2 * sl + b_: 2 * sl + b_ + 1, a - sl * WSLICE: b2 - sl * WSLICE]),
                               waits=[(sCC, cc1)], inc=sOUT, inc_n=16)
        elif stop == "p2":
            phase(0)
        else:
            phase(0)
            wqt_done = build_wqt(first=False)
            phase(1)

        P.step("sync", lambda sy: sy.nop(), waits=[(sOUT, sOUT.n), (sGP, sGP.n)])

        with nc.Block() as blk:
            blk.sync(lambda e: P.emit("sync", e))
            blk.gpsimd(lambda e: P.emit("gpsimd", e))
            blk.scalar(lambda e: P.emit("scalar", e))
            blk.vector(lambda e: P.emit("vector", e))
            blk.tensor(lambda e: P.emit("tensor", e))
    build_nc.last_prog = P
    return nc


def make_in_maps(L0_attn_X, L0_attn_Wq, L0_attn_Wk, L0_attn_Wv, L0_attn_W_G, L0_attn_W_O,
                 L0_ffn_weight, L0_c, L0_wmod):
    ident, dtp, dk, lam = host_consts()
    X = np.ascontiguousarray(np.asarray(L0_attn_X, dtype=np.float32))
    ws = {"q": L0_attn_Wq, "k": L0_attn_Wk, "v": L0_attn_Wv, "g": L0_attn_W_G, "o": L0_attn_W_O, "f": L0_ffn_weight}
    ws = {k: np.ascontiguousarray(np.asarray(v, dtype=np.float32)) for k, v in ws.items()}
    wmod = np.ascontiguousarray(np.asarray(L0_wmod, dtype=np.float32))
    cc = np.ascontiguousarray(np.asarray(L0_c, dtype=np.float32))
    in_maps = []
    for c in range(NCORES):
        m = {"x0": np.ascontiguousarray(X[0, c * T:(c + 1) * T]),
             "x1": np.ascontiguousarray(X[1, c * T:(c + 1) * T]),
             "wmods": np.ascontiguousarray(wmod[:, c * WSLICE:(c + 1) * WSLICE]),
             "cin": cc, "c_ident": ident, "c_dtp": dtp, "c_dk": dk, "c_lam": lam,
             "c_cfac": host_cfac(c),
             "c_scal": np.broadcast_to(np.array([LN_EPS, GN_EPS, FFN_EPS, 0.5, -0.5, 1.0 / 127.0, 1.0 / float(D * D), 0.25,
                                                 1.0 / float(D), 1.0 / float(DH), FFN_EPS / 127.0, 2.0], dtype=np.float32), (128, 12)).copy()}
        for k in ws:
            m[f"w_{k}"] = ws[k]
        in_maps.append(m)
    return in_maps


def kernel(**inputs):
    in_maps = make_in_maps(**inputs)
    nc = build_nc()
    res = run_bass_kernel_spmd(nc, in_maps, core_ids=list(range(NCORES)))
    out = np.empty((B, S, D), dtype=np.float32)
    for c in range(NCORES):
        out[0, c * T:(c + 1) * T] = res.results[c]["out0"]
        out[1, c * T:(c + 1) * T] = res.results[c]["out1"]
    return out
